# version 21 (final)
# baseline (speedup 1.0000x reference)
import numpy as np
import concourse.bass as bass
import concourse.mybir as mybir
from concourse.bass_utils import run_bass_kernel_spmd

F32 = mybir.dt.float32
BF16 = mybir.dt.bfloat16
AF = mybir.ActivationFunctionType
ALU = mybir.AluOpType
AX = mybir.AxisListType.X

LAMBDA_INIT = 0.8 - 0.6 * float(np.exp(-0.3 * 12))


def default_cfg():
    return dict(S=2048, D=2048, H=8, INTER=5632, B=4)


def derived(cfg):
    S, D, H, INTER = cfg["S"], cfg["D"], cfg["H"], cfg["INTER"]
    assert D == 256 * H
    d = dict(S=S, D=D, H=H, INTER=INTER, TS=S // 128, TO=S // 256, OWN=S // 2,
             KD=D // 128, NIT=INTER // 128, CHUNK=8, SCALE=128.0 ** -0.5)
    d["NCH"] = (d["NIT"] + d["CHUNK"] - 1) // d["CHUNK"]
    return d


class Sem:
    def __init__(self, nc, name):
        self.h = nc.alloc_semaphore(name)
        self.n = 0

    def bump(self, k=1):
        self.n += k
        return self.n


def build_nc(cfg):
    dv = derived(cfg)
    S, D, H, INTER = dv["S"], dv["D"], dv["H"], dv["INTER"]
    TS, TO, OWN, KD, NIT, CHUNK, NCH = (dv["TS"], dv["TO"], dv["OWN"], dv["KD"],
                                        dv["NIT"], dv["CHUNK"], dv["NCH"])
    SCALE = dv["SCALE"]
    NQC = OWN // 512
    NSC = S // 512

    nc = bass.Bass("TRN2", target_bir_lowering=False, debug=False)

    x_own = nc.dram_tensor("x_own", [OWN, D], F32, kind="ExternalInput").ap()
    x_kv = nc.dram_tensor("x_kv", [S, D], F32, kind="ExternalInput").ap()
    wq = nc.dram_tensor("wq", [D, D], F32, kind="ExternalInput").ap()
    wk = nc.dram_tensor("wk", [D, D // 2], F32, kind="ExternalInput").ap()
    wv = nc.dram_tensor("wv", [D, D // 2], F32, kind="ExternalInput").ap()
    wo = nc.dram_tensor("wo", [D, D], F32, kind="ExternalInput").ap()
    gate_wT = nc.dram_tensor("gate_wT", [D, INTER], F32, kind="ExternalInput").ap()
    up_wT = nc.dram_tensor("up_wT", [D, INTER], F32, kind="ExternalInput").ap()
    down_wT = nc.dram_tensor("down_wT", [INTER, D], F32, kind="ExternalInput").ap()
    gate_s2 = nc.dram_tensor("gate_s2", [128, NIT], F32, kind="ExternalInput").ap()
    up_s2 = nc.dram_tensor("up_s2", [128, NIT], F32, kind="ExternalInput").ap()
    down_s2 = nc.dram_tensor("down_s2", [128, KD], F32, kind="ExternalInput").ap()
    lam_row = nc.dram_tensor("lam_row", [1, 512], F32, kind="ExternalInput").ap()
    ones_row = nc.dram_tensor("ones_row", [1, 128], F32, kind="ExternalInput").ap()
    subln_w = nc.dram_tensor("subln_w", [128, 256], F32, kind="ExternalInput").ap()
    ident_in = nc.dram_tensor("ident_in", [128, 128], F32, kind="ExternalInput").ap()
    masks_in = nc.dram_tensor("masks_in", [128, TO * 512], BF16, kind="ExternalInput").ap()
    vones_in = nc.dram_tensor("vones_in", [128, TS], BF16, kind="ExternalInput").ap()
    out_own = nc.dram_tensor("out_own", [OWN, D], F32, kind="ExternalOutput").ap()
    lam_scr = nc.dram_tensor("lam_scr", [1, 128], F32, kind="Internal").ap()
    NP = H // 2
    PAIRW = 2 * S + TS * 256
    cc_mine = [nc.dram_tensor(f"cc_mine{p}", [128, PAIRW], BF16, kind="Internal").ap()
               for p in range(NP)]
    cc_both = [nc.dram_tensor(f"cc_both{p}", [256, PAIRW], BF16, kind="Internal").ap()
               for p in range(NP)]
    PAIR_GROUPS = [[2 * i, 2 * i + 1] for i in range(cfg["B"])]

    AL = lambda x: (x + 31) & ~31
    off = 18432

    def at(name, shape, dt, o):
        return nc.alloc_sbuf_tensor_at(name, shape, dt, offset=o).ap()

    def nxt(name, shape, dt):
        nonlocal off
        t = at(name, shape, dt, off)
        sz = 1
        for s in shape[1:]:
            sz *= s
        off = AL(off + sz * mybir.dt.size(dt))
        return t

    ident = nxt("ident", [128, 128], F32)
    W2 = nxt("W2", [128, 256], F32)
    lamr = nxt("lamr", [1, 512], F32)
    lam_tmp = nxt("lam_tmp", [1, 320], F32)
    lamneg = nxt("lamneg", [128, 8], F32)
    gate_sc = nxt("gate_sc", [128, NIT], F32)
    up_sc = nxt("up_sc", [128, NIT], F32)
    down_sc = nxt("down_sc", [128, KD], F32)
    vones = nxt("vones", [128, TS], BF16)
    st_sum = nxt("st_sum", [128, 24], F32)
    st_ssq = nxt("st_ssq", [128, 24], F32)
    st_m = nxt("st_m", [128, 24], F32)
    st_t1 = nxt("st_t1", [128, 24], F32)
    st_rstd = nxt("st_rstd", [128, 24], F32)
    st_nmr = nxt("st_nmr", [128, 24], F32)
    sm_rr1 = nxt("sm_rr1", [128, 8], F32)
    sm_rr2 = nxt("sm_rr2", [128, 8], F32)
    sm_rr2l = nxt("sm_rr2l", [128, 8], F32)
    sm_ms = nxt("sm_ms", [128, 8], F32)
    sm_rst = nxt("sm_rst", [128, 8], F32)
    t_comb = nxt("t_comb", [128, 256], F32)
    t2buf = nxt("t2buf", [128, 256], F32)
    a_out = nxt("a_out", [128, 512], F32)
    silu_g = nxt("silu_g", [128, OWN], F32)
    assert off <= 18432 + 16 * 1024, f"P2 overflow {off}"

    P0 = off
    xnT_own = at("xnT_own", [128, KD * OWN], BF16, P0)
    fx_stage = at("fx_stage", [128, 2 * D], F32, P0)
    fo_stage = at("fo_stage", [128, 2 * D], F32, P0 + 2 * D * 4)
    off = AL(P0 + max(KD * OWN * 2, 4 * D * 4))

    P1 = off
    aT = at("aT", [128, KD * OWN], BF16, P1)
    x_stage = at("x_stage", [128, 2 * D], F32, P1)
    xn_nat = at("xn_nat", [128, 2 * D], F32, P1 + 2 * D * 4)
    off = AL(P1 + max(KD * OWN * 2, 4 * D * 4))

    RA = off
    xnT_full = at("xnT_full", [128, KD * S], BF16, RA)
    gate_sb = at("gate_sb", [128, 2 * KD * 128], BF16, RA)
    up_sb = at("up_sb", [128, 2 * KD * 128], BF16, RA + 2 * KD * 128 * 2)
    down_sb = at("down_sb", [128, CHUNK * D], BF16, RA + 4 * KD * 128 * 2)
    hdn = at("hdn", [128, CHUNK * OWN], BF16, RA + 4 * KD * 128 * 2 + CHUNK * D * 2)
    g_sb = at("g_sb", [128, TO * D], F32, RA)
    sizeA = max(KD * S * 2, 4 * KD * 128 * 2 + CHUNK * D * 2 + CHUNK * OWN * 2,
                TO * D * 4)
    off = AL(RA + sizeA)

    RB = off
    wq_sb = at("wq_sb", [128, KD * 256], BF16, RB)
    wk_sb = at("wk_sb", [128, KD * 256], BF16, RB + KD * 256 * 2)
    wv_sb = at("wv_sb", [128, KD * 256], BF16, RB + 2 * KD * 256 * 2)
    RB2 = RB + 3 * KD * 256 * 2
    qT = at("qT", [128, 2 * OWN], BF16, RB2)
    kT = at("kT", [128, 2 * S], BF16, RB2 + 2 * OWN * 2)
    vbuf = at("vbuf", [128, TS * 264], BF16, RB2 + 2 * OWN * 2 + 2 * S * 2)
    p_buf = at("p_buf", [128, 2 * TS * 128], BF16,
               RB2 + 2 * OWN * 2 + 2 * S * 2 + TS * 264 * 2)
    masks_off = RB2 + 2 * OWN * 2 + 2 * S * 2 + TS * 264 * 2 + 2 * TS * 128 * 2
    masks = at("masks", [128, TO * 512], BF16, masks_off)
    x_stage2 = at("x_stage2", [128, D], F32, masks_off)
    cc_off = masks_off + max(TO * 512 * 2, D * 4)
    cc_stage = at("cc_stage", [128, 1024], BF16, cc_off)
    out_accT = at("out_accT", [128, KD * OWN], F32, RB)
    wo_sb = at("wo_sb", [128, KD * D], BF16, RB)
    szB_at = (3 * KD * 256 * 2 + 2 * OWN * 2 + 2 * S * 2 + TS * 264 * 2 +
              2 * TS * 128 * 2 + max(TO * 512 * 2, D * 4) + 2048)
    off = AL(RB + max(szB_at, KD * OWN * 4, KD * D * 2))
    assert off <= 229376, f"SBUF overflow: {off}"

    psA = nc.alloc_psum_tensor("psA", [128, 2048], F32).ap()
    psB = nc.alloc_psum_tensor("psB", [128, 2048], F32).ap()

    sT = Sem(nc, "sT")
    sV = Sem(nc, "sV")
    sS = Sem(nc, "sS")
    sCONST = Sem(nc, "sCONST")
    sLAM = Sem(nc, "sLAM")
    sMSK = Sem(nc, "sMSK")
    sX = [Sem(nc, "sX0"), Sem(nc, "sX1")]
    sX3 = sX + [Sem(nc, "sX2")]
    sWQ = Sem(nc, "sWQ")
    sWK = Sem(nc, "sWK")
    sWV = Sem(nc, "sWV")
    sGU = [Sem(nc, "sGU0"), Sem(nc, "sGU1")]
    sDN = Sem(nc, "sDN")
    sWO = [Sem(nc, f"sWO{i}") for i in range(max(1, KD // 4))]
    sCC = Sem(nc, "sCC")
    sCCD = Sem(nc, "sCCD")
    sSTG = [Sem(nc, "sSTG0"), Sem(nc, "sSTG1")]
    sKV0 = Sem(nc, "sKV0")
    sKV = Sem(nc, "sKV")
    sOUT = [Sem(nc, "sOUT0"), Sem(nc, "sOUT1")]

    ops = {k: [] for k in ("tensor", "vector", "scalar", "sync", "gpsimd")}

    def E(eng, fn):
        ops[eng].append(fn)

    def wait(eng, sem, val):
        if val <= 0:
            return
        assert val <= sem.n, f"wait ahead of schedule on {eng}: {val} > {sem.n}"
        E(eng, lambda e, h=sem.h, v=val: e.wait_ge(h, v))

    def dma(eng, sem, dst, src):
        c = sem.bump(16)
        E(eng, lambda e, d=dst, s=src, h=sem.h: e.dma_start(out=d, in_=s).then_inc(h, 16))
        return c

    def MM(out, lhsT, rhs, start, stop, inc=False):
        cc = sT.bump() if inc else None
        E("tensor", lambda tn, o=out, l=lhsT, r=rhs, st=start, sp=stop, cc=cc,
          hh=sT.h: (lambda i: i.then_inc(hh, 1) if cc else None)(
              tn.matmul(o, l, r, start=st, stop=sp, skip_group_check=True)))
        return cc

    def TP(out, in_, inc=False):
        cc = sT.bump() if inc else None
        E("tensor", lambda tn, o=out, i_=in_, cc=cc, hh=sT.h:
          (lambda i: i.then_inc(hh, 1) if cc else None)(
              tn.transpose(o, i_, ident[:, :])))
        return cc

    dma("sync", sCONST, ident[:, :], ident_in)
    dma("sync", sCONST, W2[:, :], subln_w)
    dma("sync", sCONST, lamr[:, :], lam_row)
    dma("sync", sCONST, gate_sc[:, :], gate_s2)
    dma("sync", sCONST, up_sc[:, :], up_s2)
    dma("sync", sCONST, down_sc[:, :], down_s2)
    dma("sync", sCONST, vones[:, :], vones_in)
    dma("sync", sCONST, lam_tmp[:1, 16:144], ones_row)
    c_consts = sCONST.n

    wq_r = wq.rearrange("(kt p) n -> p kt n", p=128)
    wk_r = wk.rearrange("(kt p) n -> p kt n", p=128)
    wv_r = wv.rearrange("(kt p) n -> p kt n", p=128)

    def w_slice_dma(dst, src_r, idx, sem):
        return (sem, dma("gpsimd", sem, dst.rearrange("p (kt c) -> p kt c", c=256),
                         src_r[:, :, idx * 256:(idx + 1) * 256]))

    c_wq = [None] * H
    c_wkv = [None] * max(1, NP)
    c_wq[0] = w_slice_dma(wq_sb, wq_r, 0, sWQ)
    c_wkv[0] = (w_slice_dma(wk_sb, wk_r, 0, sWK), w_slice_dma(wv_sb, wv_r, 0, sWV))

    def V(fn):
        c = sV.bump()
        E("vector", lambda v, hh=sV.h: fn(v).then_inc(hh, 1))
        return c

    def SC(fn):
        c = sS.bump()
        E("scalar", lambda s, hh=sS.h: fn(s).then_inc(hh, 1))
        return c

    wait("vector", sCONST, c_consts)
    cv = V(lambda v: v.tensor_mul(lam_tmp[:1, 160:288], lamr[:1, 0:128],
                                  lamr[:1, 128:256]))
    wait("vector", sV, cv)
    cv = V(lambda v: v.reduce_sum(lam_tmp[:1, 0:1], lam_tmp[:1, 160:288], axis=AX))
    wait("scalar", sV, cv)
    cs = SC(lambda s: s.activation(lam_tmp[:1, 0:1], lam_tmp[:1, 0:1], AF.Exp))
    wait("vector", sS, cs)
    cv = V(lambda v: v.tensor_mul(lam_tmp[:1, 160:288], lamr[:1, 256:384],
                                  lamr[:1, 384:512]))
    wait("vector", sV, cv)
    cv = V(lambda v: v.reduce_sum(lam_tmp[:1, 1:2], lam_tmp[:1, 160:288], axis=AX))
    wait("scalar", sV, cv)
    cs = SC(lambda s: s.activation(lam_tmp[:1, 1:2], lam_tmp[:1, 1:2], AF.Exp))
    wait("vector", sS, cs)
    cv = V(lambda v: v.tensor_sub(lam_tmp[:1, 2:3], lam_tmp[:1, 0:1],
                                  lam_tmp[:1, 1:2]))
    wait("vector", sV, cv)
    cv = V(lambda v: v.tensor_scalar(lam_tmp[:1, 2:3], lam_tmp[:1, 2:3],
                                     1.0, LAMBDA_INIT, ALU.mult, ALU.add))
    wait("vector", sV, cv)
    cv = V(lambda v: v.tensor_scalar(lam_tmp[:1, 3:4], lam_tmp[:1, 2:3],
                                     -1.0, None, ALU.mult))
    wait("vector", sV, cv)
    cv = V(lambda v: v.tensor_scalar(lam_tmp[:1, 16:144], lam_tmp[:1, 16:144],
                                     lam_tmp[:1, 3:4], None, ALU.mult))
    cv_w2 = V(lambda v: v.tensor_scalar(W2[:, :], W2[:, :], 1.0 - LAMBDA_INIT,
                                        None, ALU.mult))
    wait("sync", sV, cv)
    c1 = dma("sync", sLAM, lam_scr, lam_tmp[:1, 16:144])
    wait("sync", sLAM, c1)
    c_lam = dma("sync", sLAM, lamneg[:, 0:1], lam_scr.rearrange("a b -> b a"))

    c_pbuf_free = [0, 0]
    c_score_slot = [0, 0]
    c_av_free = [0, 0]
    c_atp_free = [0, 0]
    c_atp_slot = [0, 0]
    c_proj_done = 0
    PS_ = {"i": 0, "wq_used": 0, "wk_used": 0, "wv_used": 0, "last_proj": 0}
    NSLOT = 4
    slot_guard = [[] for _ in range(NSLOT)]
    slot_pending = [None] * NSLOT

    def drain_slot(i):
        if slot_pending[i] is not None:
            th = slot_pending[i]
            slot_pending[i] = None
            th()

    def drain_all_slots():
        for i in range(NSLOT):
            drain_slot(i)

    def take_slot():
        i = PS_["i"]
        PS_["i"] = (PS_["i"] + 1) % NSLOT
        drain_slot(i)
        for sem, cnt in slot_guard[i]:
            wait("tensor", sem, cnt)
        return i, psA[:, i * 512:i * 512 + 512]

    cc_st = {"i": 0, "guard": [0, 0]}

    def emit_proj_round(h, kind, aa, bb, extra_s_wait=0, evac="vector",
                        defer=False, dst="local"):
        if kind == "q" and aa == 0 and bb == 0:
            semq, cwq = c_wq[h]
            wait("tensor", semq, cwq)
        if (kind == "k" and aa == 0 and bb == 0) or (kind == "v" and aa % 4 == 0):
            (semk, cwk), (semv, cwv) = c_wkv[h]
            wait("tensor", semk if kind == "k" else semv,
                 cwk if kind == "k" else cwv)
        if extra_s_wait:
            wait("tensor", sS, extra_s_wait)
        i, ps = take_slot()
        last_c = None
        for kt in range(KD):
            if kind == "q":
                l_ = wq_sb[:, (kt * 2 + aa) * 128:(kt * 2 + aa + 1) * 128]
                r_ = xnT_own[:, kt * OWN + bb * 512:kt * OWN + bb * 512 + 512]
                o_ = ps
            elif kind == "k":
                l_ = wk_sb[:, (kt * 2 + aa) * 128:(kt * 2 + aa + 1) * 128]
                r_ = xnT_full[:, kt * S + bb * 512:kt * S + bb * 512 + 512]
                o_ = ps
            else:
                l_ = xnT_full[:, kt * S + aa * 128:kt * S + (aa + 1) * 128]
                r_ = wv_sb[:, kt * 256:(kt + 1) * 256]
                o_ = ps[:, 0:256]
            last_c = MM(o_, l_, r_, start=(kt == 0), stop=(kt == KD - 1),
                        inc=(kt == KD - 1))
        PS_[{"q": "wq_used", "k": "wk_used", "v": "wv_used"}[kind]] = last_c
        PS_["last_proj"] = last_c
        if kind == "q":
            d_ = qT[:, aa * OWN + bb * 512:aa * OWN + bb * 512 + 512]
            fnv = lambda e, d=d_, ps=ps: e.tensor_scalar(d, ps, SCALE, None, ALU.mult)
            fns = lambda e, d=d_, ps=ps: e.mul(d, ps, SCALE)
            mine_off = mine_w = None
        elif kind == "k":
            d_ = kT[:, aa * S + bb * 512:aa * S + bb * 512 + 512]
            fnv = lambda e, d=d_, ps=ps: e.tensor_copy(d, ps)
            fns = lambda e, d=d_, ps=ps: e.copy(d, ps)
            mine_off, mine_w = aa * S + bb * 512, 512
        else:
            d_ = vbuf[:, aa * 264:aa * 264 + 256]
            fnv = lambda e, d=d_, ps=ps: e.tensor_copy(d, ps[:, 0:256])
            fns = lambda e, d=d_, ps=ps: e.copy(d, ps[:, 0:256])
            mine_off, mine_w = 2 * S + aa * 256, 256

        def do_evac():
            if kind != "q" and dst == "stage":
                si = cc_st["i"]
                cc_st["i"] ^= 1
                wait("scalar", sSTG[si], cc_st["guard"][si])
                wait("scalar", sT, last_c)
                stg = cc_stage[:, si * 512:si * 512 + mine_w]
                c = sS.bump()
                E("scalar", lambda s, stg=stg, ps=ps, w=mine_w, hh=sS.h:
                  s.copy(stg, ps[:, 0:w]).then_inc(hh, 1))
                slot_guard[i] = [(sS, c)]
                wait("sync", sS, c)
                cd = dma("sync", sSTG[si], cc_mine[h][:, mine_off:mine_off + mine_w], stg)
                cc_st["guard"][si] = cd
                return (sV, c)
            if evac == "vector":
                wait("vector", sT, last_c)
                c = sV.bump()
                E("vector", lambda v, fn=fnv, hh=sV.h: fn(v).then_inc(hh, 1))
                slot_guard[i] = [(sV, c)]
                return (sV, c)
            else:
                wait("scalar", sT, last_c)
                c = sS.bump()
                E("scalar", lambda s, fn=fns, hh=sS.h: fn(s).then_inc(hh, 1))
                slot_guard[i] = [(sS, c)]
                return (sS, c)
        if defer:
            return i, do_evac
        return do_evac()

    NT = TS + TO
    NG = (KD + 3) // 4
    c_ln_aff = [None] * NT
    c_xdma = [None] * NT
    tp_slot_free = [0, 0]

    def xsl(t):
        if t % 3 == 2:
            return x_stage2[:, :]
        return x_stage[:, (t % 3) * D:(t % 3 + 1) * D]

    def emit_xdma(t):
        src = (x_kv[t * 128:(t + 1) * 128, :] if t < TS
               else x_own[(t - TS) * 128:(t - TS + 1) * 128, :])
        if t >= 3:
            wait("sync", sV, c_ln_aff[t - 3])
        c_xdma[t] = dma("sync", sX3[t % 3], xsl(t), src)

    emit_xdma(0)
    emit_xdma(1)
    if NT > 2:
        emit_xdma(2)
    cs_sq = [None] * NT
    cv1 = [None] * NT
    cmulA = [None] * NT

    def ln_stageA(t):
        xs = xsl(t)
        xn = xn_nat[:, (t % 2) * D:(t % 2 + 1) * D]
        wait("scalar", sX3[t % 3], c_xdma[t])
        cs_sq[t] = SC(lambda s, xs=xs, xn=xn, t=t: s.activation(
            xn, xs, AF.Square, accum_out=st_ssq[:, t:t + 1]))
        wait("vector", sX3[t % 3], c_xdma[t])
        cv = V(lambda v, xs=xs, t=t: v.reduce_sum(st_sum[:, t:t + 1], xs, axis=AX))
        wait("vector", sV, cv)
        cv = V(lambda v, t=t: v.tensor_scalar(
            st_m[:, t:t + 1], st_sum[:, t:t + 1], 1.0 / D, None, ALU.mult))
        wait("vector", sV, cv)
        cmulA[t] = V(lambda v, t=t: v.tensor_mul(
            st_t1[:, t:t + 1], st_m[:, t:t + 1], st_m[:, t:t + 1]))

    cs2v = [None] * NT

    def ln_var(t):
        wait("vector", sS, cs_sq[t])
        wait("vector", sV, cmulA[t])
        cv = V(lambda v, t=t: v.scalar_tensor_tensor(
            st_t1[:, t:t + 1], st_ssq[:, t:t + 1], 1.0 / D, st_t1[:, t:t + 1],
            ALU.mult, ALU.subtract))
        wait("vector", sV, cv)
        cv1[t] = V(lambda v, t=t: v.tensor_scalar(
            st_t1[:, t:t + 1], st_t1[:, t:t + 1], 1.0, 1e-5, ALU.mult, ALU.add))

    def ln_rstd(t):
        wait("scalar", sV, cv1[t])
        cs2 = SC(lambda s, t=t: s.activation(
            st_rstd[:, t:t + 1], st_t1[:, t:t + 1], AF.Ln))
        wait("scalar", sS, cs2)
        cs2v[t] = SC(lambda s, t=t: s.activation(
            st_rstd[:, t:t + 1], st_rstd[:, t:t + 1], AF.Exp, scale=-0.5))

    def ln_affine(t):
        xs = xsl(t)
        xn = xn_nat[:, (t % 2) * D:(t % 2 + 1) * D]
        wait("vector", sS, cs2v[t])
        cv = V(lambda v, t=t: v.scalar_tensor_tensor(
            st_nmr[:, t:t + 1], st_m[:, t:t + 1], -1.0, st_rstd[:, t:t + 1],
            ALU.mult, ALU.mult))
        wait("vector", sV, cv)
        c_ln_aff[t] = V(lambda v, xs=xs, xn=xn, t=t: v.tensor_scalar(
            xn, xs, st_rstd[:, t:t + 1], st_nmr[:, t:t + 1], ALU.mult, ALU.add))

    def ln_tp(t):
        xn = xn_nat[:, (t % 2) * D:(t % 2 + 1) * D]
        wait("tensor", sV, c_ln_aff[t])
        if t == 0:
            wait("tensor", sCONST, c_consts)
        for g in range(NG):
            gslot = g % 2
            ps = psB[:, 1024 + gslot * 512:1024 + gslot * 512 + 512]
            kts = list(range(g * 4, min(g * 4 + 4, KD)))
            nk = len(kts)
            wait("tensor", sS, tp_slot_free[gslot])
            for kt in kts:
                TP(ps[:, (kt - g * 4) * 128:(kt - g * 4 + 1) * 128],
                   xn[:, kt * 128:(kt + 1) * 128], inc=(kt == kts[-1]))
            c_tp = sT.n
            wait("scalar", sT, c_tp)
            if t < TS:
                dst = xnT_full.rearrange("p (kt s) -> p kt s", s=S)[
                    :, g * 4:g * 4 + nk, t * 128:(t + 1) * 128]
            else:
                dst = xnT_own.rearrange("p (kt s) -> p kt s", s=OWN)[
                    :, g * 4:g * 4 + nk, (t - TS) * 128:(t - TS + 1) * 128]
            tp_slot_free[gslot] = sS.bump()
            E("scalar", lambda s, dst=dst, ps=ps, nk=nk, hh=sS.h: s.copy(
                dst, ps.rearrange("p (kc c) -> p kc c", c=128)[:, 0:nk, :]
            ).then_inc(hh, 1))

    def h0_emit(kind, aa, bb, ev):
        i, th = emit_proj_round(0, kind, aa, bb, extra_s_wait=ev, evac="scalar",
                                defer=True)
        slot_pending[i] = th

    kv0 = {}

    def pair0_exchange():
        drain_all_slots()
        wait("sync", sS, sS.n)
        dma("sync", sCCD, cc_mine[0][:, 0:2 * S], kT[:, :])
        c_m0 = dma("sync", sCCD,
                   cc_mine[0].rearrange("p (c w) -> p c w", w=256)[:, 2 * S // 256:, :],
                   vbuf.rearrange("p (t c) -> p t c", c=264)[:, :, 0:256])
        wait("gpsimd", sCCD, c_m0)
        sCC.bump()
        E("gpsimd", lambda g, hh=sCC.h: g.collective_compute(
            "AllGather", ALU.bypass, replica_groups=PAIR_GROUPS,
            ins=[cc_mine[0].opt()], outs=[cc_both[0].opt()]).then_inc(hh))


    def h0_rounds(tt):
        ev = sS.n
        if tt < TS:
            h0_emit("v", tt, 0, ev)
            if tt % 4 == 3:
                sc = tt // 4
                h0_emit("k", 0, sc, ev)
                h0_emit("k", 1, sc, ev)
            if tt == TS - 1:
                if NP > 1:
                    wait("gpsimd", sT, PS_["wk_used"])
                    a2 = w_slice_dma(wk_sb, wk_r, 1, sWK)
                    wait("gpsimd", sT, PS_["wv_used"])
                    a3 = w_slice_dma(wv_sb, wv_r, 1, sWV)
                    c_wkv[1] = (a2, a3)
                pair0_exchange()
        else:
            if (tt - TS) % 4 == 3:
                qc = (tt - TS) // 4
                h0_emit("q", 0, qc, ev)
                h0_emit("q", 1, qc, ev)

    c_xnT_full = None
    for t in range(NT):
        if t >= 1:
            ln_rstd(t - 1)
            ln_affine(t - 1)
            if t + 2 < NT:
                emit_xdma(t + 2)
        ln_stageA(t)
        ln_var(t)
        if t >= 1:
            ln_tp(t - 1)
            if t - 1 == TS - 1:
                c_xnT_full = sS.n
            h0_rounds(t - 1)
    ln_rstd(NT - 1)
    ln_affine(NT - 1)
    ln_tp(NT - 1)
    if c_xnT_full is None:
        c_xnT_full = sS.n
    h0_rounds(NT - 1)
    drain_all_slots()
    c_xnT_own = sS.n

    wait("gpsimd", sS, c_xnT_own)
    c_masks = dma("gpsimd", sMSK, masks[:, :], masks_in)
    wait("vector", sS, c_xnT_own)
    cv_vones = sV.bump()
    E("vector", lambda v, hh=sV.h: v.tensor_copy(
        vbuf.rearrange("p (t c) -> p t c", c=264)[:, :, 256:257],
        vones.rearrange("p (t c) -> p t c", c=1)).then_inc(hh, 1))

    def emit_combine(h, tl):
        avslot = tl % 2
        avbase = psB[:, 0:1024] if avslot == 0 else psB[:, 1024:2048]
        av1 = avbase[:, 0:512]
        av2 = avbase[:, 512:1024]
        if h == 0 and tl == 0:
            wait("vector", sLAM, c_lam)
            wait("vector", sV, cv_w2)
        cv = V(lambda v, av1=av1: v.reciprocal(sm_rr1[:, 0:1], av1[:, 256:257]))
        cv = V(lambda v, av2=av2: v.reciprocal(sm_rr2[:, 0:1], av2[:, 256:257]))
        wait("vector", sV, cv)
        cv = V(lambda v: v.tensor_mul(sm_rr2l[:, 0:1], sm_rr2[:, 0:1],
                                      lamneg[:, 0:1]))
        wait("vector", sV, cv)
        cv = V(lambda v, av1=av1: v.tensor_scalar(
            t_comb[:, :], av1[:, 0:256], sm_rr1[:, 0:1], None, ALU.mult))
        wait("vector", sV, cv)
        c_av_free[avslot] = V(lambda v, av2=av2: v.scalar_tensor_tensor(
            t2buf[:, :], av2[:, 0:256], sm_rr2l[:, 0:1], t_comb[:, :],
            ALU.mult, ALU.add))
        wait("vector", sV, c_av_free[avslot])
        cv = V(lambda v: v.tensor_mul(t_comb[:, :], t2buf[:, :], t2buf[:, :]))
        wait("vector", sV, cv)
        cv = V(lambda v: v.reduce_sum(sm_ms[:, 0:1], t_comb[:, :], axis=AX))
        wait("vector", sV, cv)
        cv_ms = V(lambda v: v.tensor_scalar(
            sm_ms[:, 0:1], sm_ms[:, 0:1], 1.0 / 256, 1e-5, ALU.mult, ALU.add))
        wait("scalar", sV, cv_ms)
        cs = SC(lambda s: s.activation(sm_rst[:, 0:1], sm_ms[:, 0:1], AF.Ln))
        wait("scalar", sS, cs)
        c_rstd = SC(lambda s: s.activation(
            sm_rst[:, 0:1], sm_rst[:, 0:1], AF.Exp, scale=-0.5))
        wait("vector", sS, c_rstd)
        wait("vector", sT, c_atp_slot[tl % 2])
        ao = a_out[:, (tl % 2) * 256:(tl % 2) * 256 + 256]
        c_aout = V(lambda v, ao=ao: v.scalar_tensor_tensor(
            ao, t2buf[:, :], sm_rst[:, 0:1], W2[:, :],
            ALU.mult, ALU.mult))
        return c_aout

    def emit_atp(h, tl, c_aout):
        atp = tl % 2
        ao = a_out[:, atp * 256:atp * 256 + 256]
        i, tp = take_slot()
        wait("tensor", sV, c_aout)
        for j in range(2):
            TP(tp[:, j * 128:(j + 1) * 128], ao[:, j * 128:(j + 1) * 128],
               inc=(j == 1))
        c_atp_slot[atp] = sT.n
        wait("scalar", sT, c_atp_slot[atp])
        dst = aT.rearrange("p (at s) -> p at s", s=OWN)[
            :, 2 * h:2 * h + 2, tl * 128:(tl + 1) * 128]
        c_atp_free[atp] = sS.bump()
        slot_guard[i] = [(sS, c_atp_free[atp])]
        E("scalar", lambda s, dst=dst, tp=tp, hh=sS.h: s.copy(
            dst, tp.rearrange("p (j c) -> p j c", c=128)[:, 0:2, :]
        ).then_inc(hh, 1))


    c_proj_evacs_s0 = sS.n
    carry = None
    pair_pend = []
    pair_pend_p = None
    c_kv_read = 0

    cc_fire = {"p": None}

    def weave_pop():
        nonlocal pair_pend_p
        spec = pair_pend.pop(0)
        i_, th_ = emit_proj_round(pair_pend_p, spec[0], spec[1], spec[2],
                                  dst="stage", defer=True)
        slot_pending[i_] = th_
        if not pair_pend:
            drain_all_slots()
            cc_fire["p"] = pair_pend_p
            pair_pend_p = None

    def fire_collective():
        p = cc_fire["p"]
        cc_fire["p"] = None
        for si in range(2):
            wait("gpsimd", sSTG[si], cc_st["guard"][si])
        sCC.bump()
        E("gpsimd", lambda g, p=p, hh=sCC.h: g.collective_compute(
            "AllGather", ALU.bypass, replica_groups=PAIR_GROUPS,
            ins=[cc_mine[p].opt()], outs=[cc_both[p].opt()]).then_inc(hh))

    for h in range(H):
        p = h // 2
        rank = h % 2
        if h > 0:
            wait("gpsimd", sT, PS_["wq_used"])
            c_wq[h] = w_slice_dma(wq_sb, wq_r, h, sWQ)
            first = True
            for j in range(2):
                for qc in range(NQC):
                    emit_proj_round(h, "q", j, qc, evac="scalar",
                                    extra_s_wait=(c_xnT_own if first else 0))
                    first = False
        c_q_evacs = sS.n
        kv_sem = sKV
        wait("sync", sCC, p + 1)
        wait("sync", sT, c_kv_read)
        dma("sync", sKV, kT[:, :],
            cc_both[p][rank * 128:(rank + 1) * 128, 0:2 * S])
        c_kv = dma("sync", sKV,
                   vbuf.rearrange("p (t c) -> p t c", c=264)[:, :, 0:256],
                   cc_both[p].rearrange("(r p) (c w) -> r p c w", p=128, w=256)[
                       rank, :, 2 * S // 256:, :])
        if rank == 0 and p + 1 < NP:
            pair_pend_p = p + 1
            pair_pend = ([(("k"), j, sc) for j in range(2) for sc in range(NSC)]
                         + [("v", skt, 0) for skt in range(TS)])
        if rank == 1 and p + 2 < NP:
            wait("gpsimd", sT, PS_["wk_used"])
            a2 = w_slice_dma(wk_sb, wk_r, p + 2, sWK)
            wait("gpsimd", sT, PS_["wv_used"])
            a3 = w_slice_dma(wv_sb, wv_r, p + 2, sWV)
            c_wkv[p + 2] = (a2, a3)
        if rank == 1 and cc_fire["p"] is not None:
            fire_collective()

        if carry is not None:
            hp, patp = carry
            if len(patp) >= 2:
                emit_atp(hp, *patp.pop(0))
            ca = emit_combine(hp, TO - 1)
            patp.append((TO - 1, ca))
            while patp:
                emit_atp(hp, *patp.pop(0))
            carry = None

        pend_atp = []
        for tl in range(TO):
            n_skt = 2 * tl + 2
            ng = (n_skt + 3) // 4
            avslot = tl % 2
            avbase = psB[:, 0:1024] if avslot == 0 else psB[:, 1024:2048]
            wait("tensor", kv_sem, c_kv)
            if h == 0:
                wait("tensor", sS, c_proj_evacs_s0)
            else:
                wait("tensor", sS, c_q_evacs)
            sc_order = [ng - 1] + list(range(ng - 1))
            c_expg2 = [[0] * ng, [0] * ng]
            cv_mask2 = [0, 0]
            av_next = [0, 0]

            def emit_scores(beta, g):
                pb = p_buf[:, beta * TS * 128:(beta + 1) * TS * 128]
                blocks = list(range(4 * g, min(4 * g + 4, n_skt)))
                i, ps = take_slot()
                cc = None
                for b in blocks:
                    cc = MM(ps[:, (b - blocks[0]) * 128:(b - blocks[0] + 1) * 128],
                            kT[:, beta * S + b * 128:beta * S + (b + 1) * 128],
                            qT[:, beta * OWN + tl * 128:beta * OWN + (tl + 1) * 128],
                            start=True, stop=True, inc=(b == blocks[-1]))
                w = len(blocks) * 128
                wait("scalar", sT, cc)
                c_exp = sS.bump()
                E("scalar", lambda s, ps=ps, pb=pb, g=g, w=w, hh=sS.h:
                  s.activation(pb[:, g * 512:g * 512 + w], ps[:, 0:w],
                               AF.Exp).then_inc(hh, 1))
                slot_guard[i] = [(sS, c_exp)]
                c_expg2[beta][g] = c_exp
                if g == ng - 1:
                    wait("vector", sS, c_exp)
                    if h == 0 and tl == 0 and beta == 0:
                        wait("vector", sMSK, c_masks)
                    cv_mask2[beta] = sV.bump()
                    E("vector", lambda v, pb=pb, g=g, w=w, tl=tl, hh=sV.h:
                      v.tensor_mul(pb[:, g * 512:g * 512 + w],
                                   pb[:, g * 512:g * 512 + w],
                                   masks[:, tl * 512:tl * 512 + w]).then_inc(hh, 1))

            def emit_av_group(beta):
                g = av_next[beta]
                av_next[beta] += 1
                pb = p_buf[:, beta * TS * 128:(beta + 1) * TS * 128]
                avs = avbase[:, beta * 512:beta * 512 + 512]
                blocks = list(range(4 * g, min(4 * g + 4, n_skt)))
                wait("tensor", sS, c_expg2[beta][g])
                if g == 0 and beta == 0:
                    wait("tensor", sV, c_av_free[avslot])
                if g == ng - 1:
                    wait("tensor", sV, cv_mask2[beta])
                for b in blocks:
                    MM(avs[:, 0:257], pb[:, b * 128:(b + 1) * 128],
                       vbuf[:, b * 264:b * 264 + 257],
                       start=(b == 0), stop=(b == n_skt - 1),
                       inc=(b == n_skt - 1))

            for g in sc_order:
                emit_scores(0, g)
            if pair_pend:
                weave_pop()
            if pair_pend:
                weave_pop()
            for idx, g in enumerate(sc_order):
                emit_scores(1, g)
                if idx >= 1 and av_next[0] < ng:
                    emit_av_group(0)
            if pair_pend:
                weave_pop()
            if pair_pend:
                weave_pop()
            while av_next[0] < ng:
                emit_av_group(0)
            while av_next[1] < ng:
                emit_av_group(1)

            if len(pend_atp) >= 2:
                emit_atp(h, *pend_atp.pop(0))
            if tl > 0:
                ca = emit_combine(h, tl - 1)
                pend_atp.append((tl - 1, ca))
        c_kv_read = sT.n
        while pair_pend:
            weave_pop()
        drain_all_slots()
        carry = (h, pend_atp)
    hp, patp = carry
    if len(patp) >= 2:
        emit_atp(hp, *patp.pop(0))
    ca = emit_combine(hp, TO - 1)
    patp.append((TO - 1, ca))
    while patp:
        emit_atp(hp, *patp.pop(0))
    carry = None

    c_proj_done = PS_["last_proj"]

    gate_r = gate_wT.rearrange("(kt p) i -> p kt i", p=128)
    up_r = up_wT.rearrange("(kt p) i -> p kt i", p=128)
    down_r = down_wT.rearrange("(t p) hh -> p t hh", p=128)

    c_gu_free = [0, 0]
    c_dn_free = 0
    c_hdn_free = 0
    c_gu_psum_free = [0, 0]
    c_acc_hpt = [0] * KD
    c_sig_free = 0
    c_dw_free = [0, 0]

    c_gu_dma = [None] * NIT

    def gu_dma(it):
        buf = it % 2
        if it >= 2:
            wait("gpsimd", sT, c_gu_free[buf])
        else:
            wait("gpsimd", sT, c_proj_done)
        dma("gpsimd", sGU[buf],
            gate_sb.rearrange("p (b kt c) -> p b kt c", b=2, c=128)[:, buf],
            gate_r[:, :, it * 128:(it + 1) * 128])
        c_gu_dma[it] = dma("gpsimd", sGU[buf],
                           up_sb.rearrange("p (b kt c) -> p b kt c", b=2, c=128)[:, buf],
                           up_r[:, :, it * 128:(it + 1) * 128])

    for ch in range(NCH):
        it0 = ch * CHUNK
        nt = min(CHUNK, NIT - it0)
        if ch == 0:
            gu_dma(it0)
            if nt > 1:
                gu_dma(it0 + 1)
        wait("gpsimd", sT, c_proj_done if ch == 0 else c_dn_free)
        c_dn_dma = dma("gpsimd", sDN,
                       down_sb.rearrange("p (t hh) -> p t hh", hh=D)[:, 0:nt, :],
                       down_r[:, it0:it0 + nt, :])
        for itl in range(nt):
            it = it0 + itl
            buf = it % 2
            gps = psA if buf == 0 else psB
            wait("tensor", sGU[buf], c_gu_dma[it])
            wait("tensor", sV, c_gu_psum_free[buf])
            if it == 0:
                wait("tensor", sS, c_xnT_own)
                for gd in slot_guard:
                    for sem_, cnt_ in gd:
                        wait("tensor", sem_, cnt_)
                wait("tensor", sV, max(c_av_free))
            if it == 1:
                wait("tensor", sS, max(c_atp_free))
            last_g = last_u = None
            for half in range(2):
                tgt = gps[:, half * OWN:(half + 1) * OWN]
                src_w = gate_sb if half == 0 else up_sb
                for kt in range(KD):
                    for sqh in range(NQC):
                        sp = (kt == KD - 1 and sqh == NQC - 1)
                        MM(tgt[:, sqh * 512:(sqh + 1) * 512],
                           src_w[:, (buf * KD + kt) * 128:(buf * KD + kt + 1) * 128],
                           xnT_own[:, kt * OWN + sqh * 512:kt * OWN + sqh * 512 + 512],
                           start=(kt == 0), stop=sp, inc=sp)
                if half == 0:
                    last_g = sT.n
                else:
                    last_u = sT.n
            c_gu_free[buf] = last_u
            wait("scalar", sT, last_g)
            wait("scalar", sV, c_sig_free)
            cs_sig = sS.bump()
            E("scalar", lambda s, gps=gps, it=it, hh=sS.h: s.activation(
                silu_g[:, :], gps[:, 0:OWN], AF.Silu,
                scale=gate_sc[:, it:it + 1]).then_inc(hh, 1))
            wait("vector", sT, last_u)
            wait("vector", sS, cs_sig)
            if itl == 0:
                wait("vector", sT, c_hdn_free)
            c_sig_free = sV.bump()
            c_gu_psum_free[buf] = c_sig_free
            E("vector", lambda v, gps=gps, it=it, itl=itl, hh=sV.h:
              v.scalar_tensor_tensor(
                  hdn[:, itl * OWN:(itl + 1) * OWN], gps[:, OWN:2 * OWN],
                  up_sc[:, it:it + 1], silu_g[:, :],
                  ALU.mult, ALU.mult).then_inc(hh, 1))
            if it + 2 < NIT:
                gu_dma(it + 2)
        c_hdn_all = c_sig_free
        wait("tensor", sV, c_hdn_all)
        wait("tensor", sDN, c_dn_dma)
        for hpt in range(KD):
            pslot = hpt % 2
            dw = (psA if pslot == 0 else psB)[:, 0:OWN]
            wait("tensor", sV, max(c_dw_free[pslot], c_gu_psum_free[pslot]))
            last_c = None
            for itl in range(nt):
                for sqh in range(NQC):
                    sp = (itl == nt - 1 and sqh == NQC - 1)
                    last_c = MM(
                        dw[:, sqh * 512:(sqh + 1) * 512],
                        down_sb[:, itl * D + hpt * 128:itl * D + (hpt + 1) * 128],
                        hdn[:, itl * OWN + sqh * 512:itl * OWN + sqh * 512 + 512],
                        start=(itl == 0), stop=sp, inc=sp)
            wait("vector", sT, last_c)
            if ch > 0:
                wait("vector", sV, c_acc_hpt[hpt])
            dst = out_accT[:, hpt * OWN:(hpt + 1) * OWN]
            c_dw_free[pslot] = sV.bump()
            c_acc_hpt[hpt] = c_dw_free[pslot]
            if ch == 0:
                E("vector", lambda v, dst=dst, dw=dw, hpt=hpt, hh=sV.h:
                  v.tensor_scalar(dst, dw, down_sc[:, hpt:hpt + 1], None,
                                  ALU.mult).then_inc(hh, 1))
            else:
                E("vector", lambda v, dst=dst, dw=dw, hpt=hpt, hh=sV.h:
                  v.scalar_tensor_tensor(dst, dw, down_sc[:, hpt:hpt + 1], dst,
                                         ALU.mult, ALU.add).then_inc(hh, 1))
        c_gu_psum_free[0] = c_dw_free[0]
        c_gu_psum_free[1] = c_dw_free[1]
        c_dn_free = sT.n
        c_hdn_free = sT.n
    c_ffn_done = sV.n

    wo_r = wo.rearrange("(kt p) n -> p kt n", p=128)
    c_wo = [None] * (KD // 4)
    gb_slot_free = [0, 0]
    wait("tensor", sV, c_ffn_done)
    for g4 in range(KD // 4):
        for tl in range(TO):
            gslot = (g4 * TO + tl) % 2
            ps = psB[:, gslot * 512:gslot * 512 + 512]
            if gb_slot_free[gslot] > 0:
                wait("tensor", sS, gb_slot_free[gslot])
            for kk in range(4):
                hpt = g4 * 4 + kk
                TP(ps[:, kk * 128:(kk + 1) * 128],
                   out_accT[:, hpt * OWN + tl * 128:hpt * OWN + (tl + 1) * 128],
                   inc=(kk == 3))
            ctp = sT.n
            wait("scalar", sT, ctp)
            gb_slot_free[gslot] = sS.bump()
            E("scalar", lambda s, tl=tl, g4=g4, ps=ps, hh=sS.h: s.activation(
                g_sb[:, tl * D + g4 * 512:tl * D + (g4 + 1) * 512], ps,
                AF.Silu).then_inc(hh, 1))
        wait("gpsimd", sS, gb_slot_free[(g4 * TO + TO - 1) % 2])
        c_wo[g4] = dma("gpsimd", sWO[g4],
                       wo_sb.rearrange("p (kt n) -> p kt n", n=D)[:, g4 * 4:(g4 + 1) * 4, :],
                       wo_r[:, g4 * 4:(g4 + 1) * 4, :])
    c_g_done = sS.n


    c_fx = [None] * TO
    c_fcomb = [None] * TO
    c_store = [None] * TO
    c_fout_read = [0, 0]

    def emit_fx(tl):
        if tl >= 2:
            wait("sync", sV, c_fcomb[tl - 2])
        else:
            wait("sync", sS, c_g_done)
        c_fx[tl] = dma("sync", sX[tl % 2], fx_stage[:, (tl % 2) * D:(tl % 2 + 1) * D],
                       x_own[tl * 128:(tl + 1) * 128, :])

    emit_fx(0)
    if TO > 1:
        emit_fx(1)
    NFB = D // 512
    c_fout_read = [0, 0, 0, 0]
    for tl in range(TO):
        for fb in range(NFB):
            slot = fb % 4
            ps = psA[:, slot * 512:(slot + 1) * 512]
            if tl == 0 and fb == 0:
                wait("tensor", sS, c_g_done)
            wait("tensor", sV, c_fout_read[slot])
            last_c = None
            for at_ in range(KD):
                if at_ % 4 == 0 and fb == 0:
                    wait("tensor", sWO[at_ // 4], c_wo[at_ // 4])
                last_c = MM(
                    ps,
                    aT[:, at_ * OWN + tl * 128:at_ * OWN + (tl + 1) * 128],
                    wo_sb[:, at_ * D + fb * 512:at_ * D + fb * 512 + 512],
                    start=(at_ == 0), stop=(at_ == KD - 1), inc=(at_ == KD - 1))
            wait("vector", sT, last_c)
            wait("vector", sX[tl % 2], c_fx[tl])
            if tl >= 2 and fb == 0:
                wait("vector", sOUT[tl % 2], c_store[tl - 2])
            xsl = fx_stage[:, (tl % 2) * D + fb * 512:(tl % 2) * D + fb * 512 + 512]
            osl = fo_stage[:, (tl % 2) * D + fb * 512:(tl % 2) * D + fb * 512 + 512]
            gsl = g_sb[:, tl * D + fb * 512:tl * D + fb * 512 + 512]
            c_fout_read[slot] = V(lambda v, ps=ps, gsl=gsl, osl=osl:
                                  v.tensor_mul(osl, ps, gsl))
            wait("vector", sV, c_fout_read[slot])
            V(lambda v, osl=osl, xsl=xsl: v.tensor_add(osl, osl, xsl))
        c_fcomb[tl] = sV.n
        wait("sync", sV, c_fcomb[tl])
        c_store[tl] = dma("sync", sOUT[tl % 2], out_own[tl * 128:(tl + 1) * 128, :],
                          fo_stage[:, (tl % 2) * D:(tl % 2 + 1) * D])
        if tl + 2 < TO:
            emit_fx(tl + 2)
    wait("sync", sOUT[0], sOUT[0].n)
    wait("sync", sOUT[1], sOUT[1].n)
    wait("gpsimd", sOUT[0], sOUT[0].n)
    wait("gpsimd", sOUT[1], sOUT[1].n)

    with nc.Block() as block:
        @block.sync
        def _(e):
            for f in ops["sync"]:
                f(e)

        @block.gpsimd
        def _(e):
            for f in ops["gpsimd"]:
                f(e)

        @block.vector
        def _(e):
            for f in ops["vector"]:
                f(e)

        @block.scalar
        def _(e):
            for f in ops["scalar"]:
                f(e)

        @block.tensor
        def _(e):
            for f in ops["tensor"]:
                f(e)

    return nc



def make_masks(cfg, half):
    import ml_dtypes
    dv = derived(cfg)
    TO = dv["TO"]
    m = np.zeros((128, TO * 512), dtype=np.float32)
    for tl in range(TO):
        n_skt = 2 * tl + 2
        g_last = (n_skt - 1) // 4
        w = (n_skt - g_last * 4) * 128
        t = 2 * tl + half
        blk = np.zeros((128, 512), dtype=np.float32)
        for bi in range(g_last * 4, n_skt):
            col = (bi - g_last * 4) * 128
            if bi < t:
                blk[:, col:col + 128] = 1.0
            elif bi == t:
                tri = (np.arange(128)[:, None] <= np.arange(128)[None, :])
                blk[:, col:col + 128] = tri.astype(np.float32)
        m[:, tl * 512:tl * 512 + w] = blk[:, :w]
    return m.astype(ml_dtypes.bfloat16)


def make_in_maps(cfg, inputs):
    import ml_dtypes
    dv = derived(cfg)
    TS, TO, KD, NIT = dv["TS"], dv["TO"], dv["KD"], dv["NIT"]
    B = cfg["B"]

    H = cfg["H"]
    x = np.ascontiguousarray(np.asarray(inputs["L0_attn_x"], dtype=np.float32))
    wk_full = np.asarray(inputs["L0_attn_Wk"], np.float32)
    wv_full = np.asarray(inputs["L0_attn_Wv"], np.float32)

    def half_cols(w, half):
        return np.ascontiguousarray(np.concatenate(
            [w[:, (2 * p + half) * 256:(2 * p + half + 1) * 256]
             for p in range(H // 2)], axis=1))

    shared = dict(
        wq=np.ascontiguousarray(np.asarray(inputs["L0_attn_Wq"], np.float32)),
        wo=np.ascontiguousarray(np.asarray(inputs["L0_attn_Wo"], np.float32)),
        gate_wT=np.ascontiguousarray(np.asarray(inputs["L0_ffn_gate_w"], np.float32).T),
        up_wT=np.ascontiguousarray(np.asarray(inputs["L0_ffn_up_w"], np.float32).T),
        down_wT=np.ascontiguousarray(np.asarray(inputs["L0_ffn_down_w"], np.float32).T),
        gate_s2=np.ascontiguousarray(
            np.asarray(inputs["L0_ffn_gate_s"], np.float32).reshape(NIT, 128).T),
        up_s2=np.ascontiguousarray(
            np.asarray(inputs["L0_ffn_up_s"], np.float32).reshape(NIT, 128).T),
        down_s2=np.ascontiguousarray(
            np.asarray(inputs["L0_ffn_down_s"], np.float32).reshape(KD, 128).T),
        lam_row=np.concatenate([
            np.asarray(inputs["L0_attn_lambda_q1"], np.float32),
            np.asarray(inputs["L0_attn_lambda_k1"], np.float32),
            np.asarray(inputs["L0_attn_lambda_q2"], np.float32),
            np.asarray(inputs["L0_attn_lambda_k2"], np.float32)]).reshape(1, 512),
        ones_row=np.ones((1, 128), dtype=np.float32),
        subln_w=np.ascontiguousarray(np.broadcast_to(
            np.asarray(inputs["L0_attn_subln_weight"], np.float32)[None, :],
            (128, 256))),
        ident_in=np.eye(128, dtype=np.float32),
        vones_in=np.ones((128, TS), dtype=ml_dtypes.bfloat16),
    )
    masks_by_half = [make_masks(cfg, 0), make_masks(cfg, 1)]

    in_maps = []
    for c in range(2 * B):
        b, half = c // 2, c % 2
        own_rows = np.concatenate(
            [x[b, (2 * tl + half) * 128:(2 * tl + half + 1) * 128, :]
             for tl in range(TO)], axis=0)
        m = dict(shared)
        m["x_own"] = np.ascontiguousarray(own_rows)
        m["x_kv"] = np.ascontiguousarray(x[b])
        m["masks_in"] = masks_by_half[half]
        m["wk"] = half_cols(wk_full, half)
        m["wv"] = half_cols(wv_full, half)
        in_maps.append(m)
    return in_maps


def assemble_out(cfg, results):
    dv = derived(cfg)
    S, D, TO = dv["S"], dv["D"], dv["TO"]
    B = cfg["B"]
    out = np.empty((B, S, D), dtype=np.float32)
    for c in range(2 * B):
        b, half = c // 2, c % 2
        r = results[c]["out_own"]
        for tl in range(TO):
            out[b, (2 * tl + half) * 128:(2 * tl + half + 1) * 128, :] = \
                r[tl * 128:(tl + 1) * 128, :]
    return out


_CACHE = {}


def kernel(**inputs):
    cfg = default_cfg()
    if "nc" not in _CACHE:
        _CACHE["nc"] = build_nc(cfg)
    in_maps = make_in_maps(cfg, inputs)
    res = run_bass_kernel_spmd(_CACHE["nc"], in_maps, core_ids=list(range(8)))
    return assemble_out(cfg, res.results)


# version 22
# speedup vs baseline: 1.1293x; 1.1293x over previous
import numpy as np
import concourse.bass as bass
import concourse.mybir as mybir
from concourse.bass_utils import run_bass_kernel_spmd

F32 = mybir.dt.float32
BF16 = mybir.dt.bfloat16
AF = mybir.ActivationFunctionType
ALU = mybir.AluOpType
AX = mybir.AxisListType.X

LAMBDA_INIT = 0.8 - 0.6 * float(np.exp(-0.3 * 12))


def default_cfg():
    return dict(S=2048, D=2048, H=8, INTER=5632, B=4)


def derived(cfg):
    S, D, H, INTER = cfg["S"], cfg["D"], cfg["H"], cfg["INTER"]
    assert D == 256 * H
    d = dict(S=S, D=D, H=H, INTER=INTER, TS=S // 128, TO=S // 256, OWN=S // 2,
             KD=D // 128, NIT=INTER // 128, CHUNK=8, SCALE=128.0 ** -0.5)
    d["NCH"] = (d["NIT"] + d["CHUNK"] - 1) // d["CHUNK"]
    return d


class Sem:
    def __init__(self, nc, name):
        self.h = nc.alloc_semaphore(name)
        self.n = 0

    def bump(self, k=1):
        self.n += k
        return self.n


def build_nc(cfg):
    dv = derived(cfg)
    S, D, H, INTER = dv["S"], dv["D"], dv["H"], dv["INTER"]
    TS, TO, OWN, KD, NIT, CHUNK, NCH = (dv["TS"], dv["TO"], dv["OWN"], dv["KD"],
                                        dv["NIT"], dv["CHUNK"], dv["NCH"])
    SCALE = dv["SCALE"]
    NQC = OWN // 512
    NSC = S // 512

    nc = bass.Bass("TRN2", target_bir_lowering=False, debug=False)

    x_own = nc.dram_tensor("x_own", [OWN, D], F32, kind="ExternalInput").ap()
    x_kv = nc.dram_tensor("x_kv", [S, D], F32, kind="ExternalInput").ap()
    wq = nc.dram_tensor("wq", [D, D], F32, kind="ExternalInput").ap()
    wk = nc.dram_tensor("wk", [D, D // 2], F32, kind="ExternalInput").ap()
    wv = nc.dram_tensor("wv", [D, D // 2], F32, kind="ExternalInput").ap()
    wo = nc.dram_tensor("wo", [D, D], F32, kind="ExternalInput").ap()
    gate_wT = nc.dram_tensor("gate_wT", [D, INTER], F32, kind="ExternalInput").ap()
    up_wT = nc.dram_tensor("up_wT", [D, INTER], F32, kind="ExternalInput").ap()
    down_wT = nc.dram_tensor("down_wT", [INTER, D], F32, kind="ExternalInput").ap()
    gate_s2 = nc.dram_tensor("gate_s2", [128, NIT], F32, kind="ExternalInput").ap()
    up_s2 = nc.dram_tensor("up_s2", [128, NIT], F32, kind="ExternalInput").ap()
    down_s2 = nc.dram_tensor("down_s2", [128, KD], F32, kind="ExternalInput").ap()
    lam_row = nc.dram_tensor("lam_row", [1, 512], F32, kind="ExternalInput").ap()
    ones_row = nc.dram_tensor("ones_row", [1, 128], F32, kind="ExternalInput").ap()
    subln_w = nc.dram_tensor("subln_w", [128, 256], F32, kind="ExternalInput").ap()
    ident_in = nc.dram_tensor("ident_in", [128, 128], F32, kind="ExternalInput").ap()
    masks_in = nc.dram_tensor("masks_in", [128, TO * 512], BF16, kind="ExternalInput").ap()
    vones_in = nc.dram_tensor("vones_in", [128, TS], BF16, kind="ExternalInput").ap()
    out_own = nc.dram_tensor("out_own", [OWN, D], F32, kind="ExternalOutput").ap()
    lam_scr = nc.dram_tensor("lam_scr", [1, 128], F32, kind="Internal").ap()
    NP = H // 2
    PAIRW = 2 * S + TS * 256
    cc_mine = [nc.dram_tensor(f"cc_mine{p}", [128, PAIRW], BF16, kind="Internal").ap()
               for p in range(NP)]
    cc_both = [nc.dram_tensor(f"cc_both{p}", [256, PAIRW], BF16, kind="Internal").ap()
               for p in range(NP)]
    PAIR_GROUPS = [[2 * i, 2 * i + 1] for i in range(cfg["B"])]

    AL = lambda x: (x + 31) & ~31
    off = 18432

    def at(name, shape, dt, o):
        return nc.alloc_sbuf_tensor_at(name, shape, dt, offset=o).ap()

    def nxt(name, shape, dt):
        nonlocal off
        t = at(name, shape, dt, off)
        sz = 1
        for s in shape[1:]:
            sz *= s
        off = AL(off + sz * mybir.dt.size(dt))
        return t

    ident = nxt("ident", [128, 128], F32)
    W2 = nxt("W2", [128, 256], F32)
    lamr = nxt("lamr", [1, 512], F32)
    lam_tmp = nxt("lam_tmp", [1, 320], F32)
    lamneg = nxt("lamneg", [128, 8], F32)
    gate_sc = nxt("gate_sc", [128, NIT], F32)
    up_sc = nxt("up_sc", [128, NIT], F32)
    down_sc = nxt("down_sc", [128, KD], F32)
    vones = nxt("vones", [128, TS], BF16)
    st_sum = nxt("st_sum", [128, 24], F32)
    st_ssq = nxt("st_ssq", [128, 24], F32)
    st_m = nxt("st_m", [128, 24], F32)
    st_t1 = nxt("st_t1", [128, 24], F32)
    st_rstd = nxt("st_rstd", [128, 24], F32)
    st_nmr = nxt("st_nmr", [128, 24], F32)
    sm_rr1 = nxt("sm_rr1", [128, 8], F32)
    sm_rr2 = nxt("sm_rr2", [128, 8], F32)
    sm_rr2l = nxt("sm_rr2l", [128, 8], F32)
    sm_ms = nxt("sm_ms", [128, 8], F32)
    sm_rst = nxt("sm_rst", [128, 8], F32)
    t_comb = nxt("t_comb", [128, 256], F32)
    t2buf = nxt("t2buf", [128, 256], F32)
    a_out = nxt("a_out", [128, 512], F32)
    silu_g = nxt("silu_g", [128, OWN], F32)
    assert off <= 18432 + 16 * 1024, f"P2 overflow {off}"

    P0 = off
    xnT_own = at("xnT_own", [128, KD * OWN], BF16, P0)
    fx_stage = at("fx_stage", [128, 2 * D], F32, P0)
    fo_stage = at("fo_stage", [128, 2 * D], F32, P0 + 2 * D * 4)
    off = AL(P0 + max(KD * OWN * 2, 4 * D * 4))

    P1 = off
    aT = at("aT", [128, KD * OWN], BF16, P1)
    x_stage = at("x_stage", [128, 2 * D], F32, P1)
    xn_nat = at("xn_nat", [128, 2 * D], F32, P1 + 2 * D * 4)
    off = AL(P1 + max(KD * OWN * 2, 4 * D * 4))

    RA = off
    xnT_full = at("xnT_full", [128, KD * S], BF16, RA)
    gate_sb = at("gate_sb", [128, 2 * KD * 128], BF16, RA)
    up_sb = at("up_sb", [128, 2 * KD * 128], BF16, RA + 2 * KD * 128 * 2)
    down_sb = at("down_sb", [128, CHUNK * D], BF16, RA + 4 * KD * 128 * 2)
    hdn = at("hdn", [128, CHUNK * OWN], BF16, RA + 4 * KD * 128 * 2 + CHUNK * D * 2)
    g_sb = at("g_sb", [128, TO * D], F32, RA)
    sizeA = max(KD * S * 2, 4 * KD * 128 * 2 + CHUNK * D * 2 + CHUNK * OWN * 2,
                TO * D * 4)
    off = AL(RA + sizeA)

    RB = off
    wq_sb = at("wq_sb", [128, KD * 256], BF16, RB)
    wk_sb = at("wk_sb", [128, KD * 256], BF16, RB + KD * 256 * 2)
    wv_sb = at("wv_sb", [128, KD * 256], BF16, RB + 2 * KD * 256 * 2)
    RB2 = RB + 3 * KD * 256 * 2
    qT = at("qT", [128, 2 * OWN], BF16, RB2)
    kT = at("kT", [128, 2 * S], BF16, RB2 + 2 * OWN * 2)
    vbuf = at("vbuf", [128, TS * 264], BF16, RB2 + 2 * OWN * 2 + 2 * S * 2)
    p_buf = at("p_buf", [128, 2 * TS * 128], BF16,
               RB2 + 2 * OWN * 2 + 2 * S * 2 + TS * 264 * 2)
    masks_off = RB2 + 2 * OWN * 2 + 2 * S * 2 + TS * 264 * 2 + 2 * TS * 128 * 2
    masks = at("masks", [128, TO * 512], BF16, masks_off)
    x_stage2 = at("x_stage2", [128, D], F32, masks_off)
    cc_off = masks_off + max(TO * 512 * 2, D * 4)
    cc_stage = at("cc_stage", [128, 1024], BF16, cc_off)
    out_accT = at("out_accT", [128, KD * OWN], F32, RB)
    wo_sb = at("wo_sb", [128, KD * D], BF16, RB)
    szB_at = (3 * KD * 256 * 2 + 2 * OWN * 2 + 2 * S * 2 + TS * 264 * 2 +
              2 * TS * 128 * 2 + max(TO * 512 * 2, D * 4) + 2048)
    off = AL(RB + max(szB_at, KD * OWN * 4, KD * D * 2))
    assert off <= 229376, f"SBUF overflow: {off}"

    psA = nc.alloc_psum_tensor("psA", [128, 2048], F32).ap()
    psB = nc.alloc_psum_tensor("psB", [128, 2048], F32).ap()

    sT = Sem(nc, "sT")
    sV = Sem(nc, "sV")
    sS = Sem(nc, "sS")
    sCONST = Sem(nc, "sCONST")
    sLAM = Sem(nc, "sLAM")
    sMSK = Sem(nc, "sMSK")
    sX = [Sem(nc, "sX0"), Sem(nc, "sX1")]
    sX3 = sX + [Sem(nc, "sX2")]
    sWQ = Sem(nc, "sWQ")
    sWK = Sem(nc, "sWK")
    sWV = Sem(nc, "sWV")
    sGU = [Sem(nc, "sGU0"), Sem(nc, "sGU1")]
    sDN = Sem(nc, "sDN")
    sWO = [Sem(nc, f"sWO{i}") for i in range(max(1, KD // 4))]
    sCC = Sem(nc, "sCC")
    sCCD = Sem(nc, "sCCD")
    sSTG = [Sem(nc, "sSTG0"), Sem(nc, "sSTG1")]
    sKV0 = Sem(nc, "sKV0")
    sKV = Sem(nc, "sKV")
    sOUT = [Sem(nc, "sOUT0"), Sem(nc, "sOUT1")]

    ops = {k: [] for k in ("tensor", "vector", "scalar", "sync", "gpsimd")}

    def E(eng, fn):
        ops[eng].append(fn)

    def wait(eng, sem, val):
        if val <= 0:
            return
        assert val <= sem.n, f"wait ahead of schedule on {eng}: {val} > {sem.n}"
        E(eng, lambda e, h=sem.h, v=val: e.wait_ge(h, v))

    def dma(eng, sem, dst, src):
        c = sem.bump(16)
        E(eng, lambda e, d=dst, s=src, h=sem.h: e.dma_start(out=d, in_=s).then_inc(h, 16))
        return c

    def MM(out, lhsT, rhs, start, stop, inc=False):
        cc = sT.bump() if inc else None
        E("tensor", lambda tn, o=out, l=lhsT, r=rhs, st=start, sp=stop, cc=cc,
          hh=sT.h: (lambda i: i.then_inc(hh, 1) if cc else None)(
              tn.matmul(o, l, r, start=st, stop=sp, skip_group_check=True)))
        return cc

    def TP(out, in_, inc=False):
        cc = sT.bump() if inc else None
        E("tensor", lambda tn, o=out, i_=in_, cc=cc, hh=sT.h:
          (lambda i: i.then_inc(hh, 1) if cc else None)(
              tn.transpose(o, i_, ident[:, :])))
        return cc

    dma("sync", sCONST, ident[:, :], ident_in)
    dma("sync", sCONST, W2[:, :], subln_w)
    dma("sync", sCONST, lamr[:, :], lam_row)
    dma("sync", sCONST, gate_sc[:, :], gate_s2)
    dma("sync", sCONST, up_sc[:, :], up_s2)
    dma("sync", sCONST, down_sc[:, :], down_s2)
    dma("sync", sCONST, vones[:, :], vones_in)
    dma("sync", sCONST, lam_tmp[:1, 16:144], ones_row)
    c_consts = sCONST.n

    wq_r = wq.rearrange("(kt p) n -> p kt n", p=128)
    wk_r = wk.rearrange("(kt p) n -> p kt n", p=128)
    wv_r = wv.rearrange("(kt p) n -> p kt n", p=128)

    def w_slice_dma(dst, src_r, idx, sem):
        return (sem, dma("gpsimd", sem, dst.rearrange("p (kt c) -> p kt c", c=256),
                         src_r[:, :, idx * 256:(idx + 1) * 256]))

    c_wq = [None] * H
    c_wkv = [None] * max(1, NP)
    c_wq[0] = w_slice_dma(wq_sb, wq_r, 0, sWQ)
    c_wkv[0] = (w_slice_dma(wk_sb, wk_r, 0, sWK), w_slice_dma(wv_sb, wv_r, 0, sWV))

    def V(fn):
        c = sV.bump()
        E("vector", lambda v, hh=sV.h: fn(v).then_inc(hh, 1))
        return c

    def SC(fn):
        c = sS.bump()
        E("scalar", lambda s, hh=sS.h: fn(s).then_inc(hh, 1))
        return c

    wait("vector", sCONST, c_consts)
    cv = V(lambda v: v.tensor_mul(lam_tmp[:1, 160:288], lamr[:1, 0:128],
                                  lamr[:1, 128:256]))
    wait("vector", sV, cv)
    cv = V(lambda v: v.reduce_sum(lam_tmp[:1, 0:1], lam_tmp[:1, 160:288], axis=AX))
    wait("scalar", sV, cv)
    cs = SC(lambda s: s.activation(lam_tmp[:1, 0:1], lam_tmp[:1, 0:1], AF.Exp))
    wait("vector", sS, cs)
    cv = V(lambda v: v.tensor_mul(lam_tmp[:1, 160:288], lamr[:1, 256:384],
                                  lamr[:1, 384:512]))
    wait("vector", sV, cv)
    cv = V(lambda v: v.reduce_sum(lam_tmp[:1, 1:2], lam_tmp[:1, 160:288], axis=AX))
    wait("scalar", sV, cv)
    cs = SC(lambda s: s.activation(lam_tmp[:1, 1:2], lam_tmp[:1, 1:2], AF.Exp))
    wait("vector", sS, cs)
    cv = V(lambda v: v.tensor_sub(lam_tmp[:1, 2:3], lam_tmp[:1, 0:1],
                                  lam_tmp[:1, 1:2]))
    wait("vector", sV, cv)
    cv = V(lambda v: v.tensor_scalar(lam_tmp[:1, 2:3], lam_tmp[:1, 2:3],
                                     1.0, LAMBDA_INIT, ALU.mult, ALU.add))
    wait("vector", sV, cv)
    cv = V(lambda v: v.tensor_scalar(lam_tmp[:1, 3:4], lam_tmp[:1, 2:3],
                                     -1.0, None, ALU.mult))
    wait("vector", sV, cv)
    cv = V(lambda v: v.tensor_scalar(lam_tmp[:1, 16:144], lam_tmp[:1, 16:144],
                                     lam_tmp[:1, 3:4], None, ALU.mult))
    cv_w2 = V(lambda v: v.tensor_scalar(W2[:, :], W2[:, :], 1.0 - LAMBDA_INIT,
                                        None, ALU.mult))
    wait("sync", sV, cv)
    c1 = dma("sync", sLAM, lam_scr, lam_tmp[:1, 16:144])
    wait("sync", sLAM, c1)
    c_lam = dma("sync", sLAM, lamneg[:, 0:1], lam_scr.rearrange("a b -> b a"))

    c_pbuf_free = [0, 0]
    c_score_slot = [0, 0]
    c_av_free = [0, 0]
    c_atp_free = [0, 0]
    c_atp_slot = [0, 0]
    c_proj_done = 0
    PS_ = {"i": 0, "wq_used": 0, "wk_used": 0, "wv_used": 0, "last_proj": 0}
    NSLOT = 4
    slot_guard = [[] for _ in range(NSLOT)]
    slot_pending = [None] * NSLOT

    def drain_slot(i):
        if slot_pending[i] is not None:
            th = slot_pending[i]
            slot_pending[i] = None
            th()

    def drain_all_slots():
        for i in range(NSLOT):
            drain_slot(i)

    def take_slot():
        i = PS_["i"]
        PS_["i"] = (PS_["i"] + 1) % NSLOT
        drain_slot(i)
        for sem, cnt in slot_guard[i]:
            wait("tensor", sem, cnt)
        return i, psA[:, i * 512:i * 512 + 512]

    cc_st = {"i": 0, "guard": [0, 0]}

    def emit_proj_round(h, kind, aa, bb, extra_s_wait=0, evac="vector",
                        defer=False, dst="local"):
        if kind == "q" and aa == 0 and bb == 0:
            semq, cwq = c_wq[h]
            wait("tensor", semq, cwq)
        if (kind == "k" and aa == 0 and bb == 0) or (kind == "v" and aa % 4 == 0):
            (semk, cwk), (semv, cwv) = c_wkv[h]
            wait("tensor", semk if kind == "k" else semv,
                 cwk if kind == "k" else cwv)
        if extra_s_wait:
            wait("tensor", sS, extra_s_wait)
        i, ps = take_slot()
        last_c = None
        for kt in range(KD):
            if kind == "q":
                l_ = wq_sb[:, (kt * 2 + aa) * 128:(kt * 2 + aa + 1) * 128]
                r_ = xnT_own[:, kt * OWN + bb * 512:kt * OWN + bb * 512 + 512]
                o_ = ps
            elif kind == "k":
                l_ = wk_sb[:, (kt * 2 + aa) * 128:(kt * 2 + aa + 1) * 128]
                r_ = xnT_full[:, kt * S + bb * 512:kt * S + bb * 512 + 512]
                o_ = ps
            else:
                l_ = xnT_full[:, kt * S + aa * 128:kt * S + (aa + 1) * 128]
                r_ = wv_sb[:, kt * 256:(kt + 1) * 256]
                o_ = ps[:, 0:256]
            last_c = MM(o_, l_, r_, start=(kt == 0), stop=(kt == KD - 1),
                        inc=(kt == KD - 1))
        PS_[{"q": "wq_used", "k": "wk_used", "v": "wv_used"}[kind]] = last_c
        PS_["last_proj"] = last_c
        if kind == "q":
            d_ = qT[:, aa * OWN + bb * 512:aa * OWN + bb * 512 + 512]
            fnv = lambda e, d=d_, ps=ps: e.tensor_scalar(d, ps, SCALE, None, ALU.mult)
            fns = lambda e, d=d_, ps=ps: e.mul(d, ps, SCALE)
            mine_off = mine_w = None
        elif kind == "k":
            d_ = kT[:, aa * S + bb * 512:aa * S + bb * 512 + 512]
            fnv = lambda e, d=d_, ps=ps: e.tensor_copy(d, ps)
            fns = lambda e, d=d_, ps=ps: e.copy(d, ps)
            mine_off, mine_w = aa * S + bb * 512, 512
        else:
            d_ = vbuf[:, aa * 264:aa * 264 + 256]
            fnv = lambda e, d=d_, ps=ps: e.tensor_copy(d, ps[:, 0:256])
            fns = lambda e, d=d_, ps=ps: e.copy(d, ps[:, 0:256])
            mine_off, mine_w = 2 * S + aa * 256, 256

        def do_evac():
            if kind != "q" and dst == "stage":
                si = cc_st["i"]
                cc_st["i"] ^= 1
                wait("scalar", sSTG[si], cc_st["guard"][si])
                wait("scalar", sT, last_c)
                stg = cc_stage[:, si * 512:si * 512 + mine_w]
                c = sS.bump()
                E("scalar", lambda s, stg=stg, ps=ps, w=mine_w, hh=sS.h:
                  s.copy(stg, ps[:, 0:w]).then_inc(hh, 1))
                slot_guard[i] = [(sS, c)]
                wait("sync", sS, c)
                cd = dma("sync", sSTG[si], cc_mine[h][:, mine_off:mine_off + mine_w], stg)
                cc_st["guard"][si] = cd
                return (sV, c)
            if evac == "vector":
                wait("vector", sT, last_c)
                c = sV.bump()
                E("vector", lambda v, fn=fnv, hh=sV.h: fn(v).then_inc(hh, 1))
                slot_guard[i] = [(sV, c)]
                return (sV, c)
            else:
                wait("scalar", sT, last_c)
                c = sS.bump()
                E("scalar", lambda s, fn=fns, hh=sS.h: fn(s).then_inc(hh, 1))
                slot_guard[i] = [(sS, c)]
                return (sS, c)
        if defer:
            return i, do_evac
        return do_evac()

    NT = TS + TO
    NG = (KD + 3) // 4
    c_ln_aff = [None] * NT
    c_xdma = [None] * NT
    tp_slot_free = [0, 0]

    def xsl(t):
        if t % 3 == 2:
            return x_stage2[:, :]
        return x_stage[:, (t % 3) * D:(t % 3 + 1) * D]

    def emit_xdma(t):
        src = (x_kv[t * 128:(t + 1) * 128, :] if t < TS
               else x_own[(t - TS) * 128:(t - TS + 1) * 128, :])
        if t >= 3:
            wait("sync", sV, c_ln_aff[t - 3])
        c_xdma[t] = dma("sync", sX3[t % 3], xsl(t), src)

    emit_xdma(0)
    emit_xdma(1)
    if NT > 2:
        emit_xdma(2)
    cs_sq = [None] * NT
    cv1 = [None] * NT
    cmulA = [None] * NT

    def ln_stageA(t):
        xs = xsl(t)
        xn = xn_nat[:, (t % 2) * D:(t % 2 + 1) * D]
        wait("scalar", sX3[t % 3], c_xdma[t])
        cs_sq[t] = SC(lambda s, xs=xs, xn=xn, t=t: s.activation(
            xn, xs, AF.Square, accum_out=st_ssq[:, t:t + 1]))
        wait("vector", sX3[t % 3], c_xdma[t])
        cv = V(lambda v, xs=xs, t=t: v.reduce_sum(st_sum[:, t:t + 1], xs, axis=AX))
        wait("vector", sV, cv)
        cv = V(lambda v, t=t: v.tensor_scalar(
            st_m[:, t:t + 1], st_sum[:, t:t + 1], 1.0 / D, None, ALU.mult))
        wait("vector", sV, cv)
        cmulA[t] = V(lambda v, t=t: v.tensor_mul(
            st_t1[:, t:t + 1], st_m[:, t:t + 1], st_m[:, t:t + 1]))

    cs2v = [None] * NT

    def ln_var(t):
        wait("vector", sS, cs_sq[t])
        wait("vector", sV, cmulA[t])
        cv = V(lambda v, t=t: v.scalar_tensor_tensor(
            st_t1[:, t:t + 1], st_ssq[:, t:t + 1], 1.0 / D, st_t1[:, t:t + 1],
            ALU.mult, ALU.subtract))
        wait("vector", sV, cv)
        cv1[t] = V(lambda v, t=t: v.tensor_scalar(
            st_t1[:, t:t + 1], st_t1[:, t:t + 1], 1.0, 1e-5, ALU.mult, ALU.add))

    def ln_rstd(t):
        wait("scalar", sV, cv1[t])
        cs2 = SC(lambda s, t=t: s.activation(
            st_rstd[:, t:t + 1], st_t1[:, t:t + 1], AF.Ln))
        wait("scalar", sS, cs2)
        cs2v[t] = SC(lambda s, t=t: s.activation(
            st_rstd[:, t:t + 1], st_rstd[:, t:t + 1], AF.Exp, scale=-0.5))

    def ln_affine(t):
        xs = xsl(t)
        xn = xn_nat[:, (t % 2) * D:(t % 2 + 1) * D]
        wait("vector", sS, cs2v[t])
        cv = V(lambda v, t=t: v.scalar_tensor_tensor(
            st_nmr[:, t:t + 1], st_m[:, t:t + 1], -1.0, st_rstd[:, t:t + 1],
            ALU.mult, ALU.mult))
        wait("vector", sV, cv)
        c_ln_aff[t] = V(lambda v, xs=xs, xn=xn, t=t: v.tensor_scalar(
            xn, xs, st_rstd[:, t:t + 1], st_nmr[:, t:t + 1], ALU.mult, ALU.add))

    def ln_tp(t):
        xn = xn_nat[:, (t % 2) * D:(t % 2 + 1) * D]
        wait("tensor", sV, c_ln_aff[t])
        if t == 0:
            wait("tensor", sCONST, c_consts)
        for g in range(NG):
            gslot = g % 2
            ps = psB[:, 1024 + gslot * 512:1024 + gslot * 512 + 512]
            kts = list(range(g * 4, min(g * 4 + 4, KD)))
            nk = len(kts)
            wait("tensor", sS, tp_slot_free[gslot])
            for kt in kts:
                TP(ps[:, (kt - g * 4) * 128:(kt - g * 4 + 1) * 128],
                   xn[:, kt * 128:(kt + 1) * 128], inc=(kt == kts[-1]))
            c_tp = sT.n
            wait("scalar", sT, c_tp)
            if t < TS:
                dst = xnT_full.rearrange("p (kt s) -> p kt s", s=S)[
                    :, g * 4:g * 4 + nk, t * 128:(t + 1) * 128]
            else:
                dst = xnT_own.rearrange("p (kt s) -> p kt s", s=OWN)[
                    :, g * 4:g * 4 + nk, (t - TS) * 128:(t - TS + 1) * 128]
            tp_slot_free[gslot] = sS.bump()
            E("scalar", lambda s, dst=dst, ps=ps, nk=nk, hh=sS.h: s.copy(
                dst, ps.rearrange("p (kc c) -> p kc c", c=128)[:, 0:nk, :]
            ).then_inc(hh, 1))

    def h0_emit(kind, aa, bb, ev):
        i, th = emit_proj_round(0, kind, aa, bb, extra_s_wait=ev, evac="scalar",
                                defer=True)
        slot_pending[i] = th

    kv0 = {}

    def pair0_exchange():
        drain_all_slots()
        wait("sync", sS, sS.n)
        dma("sync", sCCD, cc_mine[0][:, 0:2 * S], kT[:, :])
        c_m0 = dma("sync", sCCD,
                   cc_mine[0].rearrange("p (c w) -> p c w", w=256)[:, 2 * S // 256:, :],
                   vbuf.rearrange("p (t c) -> p t c", c=264)[:, :, 0:256])
        wait("gpsimd", sCCD, c_m0)
        sCC.bump()
        E("gpsimd", lambda g, hh=sCC.h: g.collective_compute(
            "AllGather", ALU.bypass, replica_groups=PAIR_GROUPS,
            ins=[cc_mine[0].opt()], outs=[cc_both[0].opt()]).then_inc(hh))


    def h0_rounds(tt):
        ev = sS.n
        if tt < TS:
            h0_emit("v", tt, 0, ev)
            if tt % 4 == 3:
                sc = tt // 4
                h0_emit("k", 0, sc, ev)
                h0_emit("k", 1, sc, ev)
            if tt == TS - 1:
                if NP > 1:
                    wait("gpsimd", sT, PS_["wk_used"])
                    a2 = w_slice_dma(wk_sb, wk_r, 1, sWK)
                    wait("gpsimd", sT, PS_["wv_used"])
                    a3 = w_slice_dma(wv_sb, wv_r, 1, sWV)
                    c_wkv[1] = (a2, a3)
                pair0_exchange()
        else:
            if (tt - TS) % 4 == 3:
                qc = (tt - TS) // 4
                h0_emit("q", 0, qc, ev)
                h0_emit("q", 1, qc, ev)

    c_xnT_full = None
    for t in range(NT):
        if t >= 1:
            ln_rstd(t - 1)
            ln_affine(t - 1)
            if t + 2 < NT:
                emit_xdma(t + 2)
        ln_stageA(t)
        ln_var(t)
        if t >= 1:
            ln_tp(t - 1)
            if t - 1 == TS - 1:
                c_xnT_full = sS.n
            h0_rounds(t - 1)
    ln_rstd(NT - 1)
    ln_affine(NT - 1)
    ln_tp(NT - 1)
    if c_xnT_full is None:
        c_xnT_full = sS.n
    h0_rounds(NT - 1)
    drain_all_slots()
    c_xnT_own = sS.n

    wait("gpsimd", sS, c_xnT_own)
    c_masks = dma("gpsimd", sMSK, masks[:, :], masks_in)
    wait("vector", sS, c_xnT_own)
    cv_vones = sV.bump()
    E("vector", lambda v, hh=sV.h: v.tensor_copy(
        vbuf.rearrange("p (t c) -> p t c", c=264)[:, :, 256:257],
        vones.rearrange("p (t c) -> p t c", c=1)).then_inc(hh, 1))

    def emit_combine(h, tl, c_av):
        avslot = tl % 2
        avbase = psB[:, 0:1024] if avslot == 0 else psB[:, 1024:2048]
        av1 = avbase[:, 0:512]
        av2 = avbase[:, 512:1024]
        wait("vector", sT, c_av)
        if h == 0 and tl == 0:
            wait("vector", sLAM, c_lam)
            wait("vector", sV, cv_w2)
        cv = V(lambda v, av1=av1: v.reciprocal(sm_rr1[:, 0:1], av1[:, 256:257]))
        cv = V(lambda v, av2=av2: v.reciprocal(sm_rr2[:, 0:1], av2[:, 256:257]))
        wait("vector", sV, cv)
        cv = V(lambda v: v.tensor_mul(sm_rr2l[:, 0:1], sm_rr2[:, 0:1],
                                      lamneg[:, 0:1]))
        wait("vector", sV, cv)
        cv = V(lambda v, av1=av1: v.tensor_scalar(
            t_comb[:, :], av1[:, 0:256], sm_rr1[:, 0:1], None, ALU.mult))
        wait("vector", sV, cv)
        c_av_free[avslot] = V(lambda v, av2=av2: v.scalar_tensor_tensor(
            t2buf[:, :], av2[:, 0:256], sm_rr2l[:, 0:1], t_comb[:, :],
            ALU.mult, ALU.add))
        wait("vector", sV, c_av_free[avslot])
        cv = V(lambda v: v.tensor_mul(t_comb[:, :], t2buf[:, :], t2buf[:, :]))
        wait("vector", sV, cv)
        cv = V(lambda v: v.reduce_sum(sm_ms[:, 0:1], t_comb[:, :], axis=AX))
        wait("vector", sV, cv)
        cv_ms = V(lambda v: v.tensor_scalar(
            sm_ms[:, 0:1], sm_ms[:, 0:1], 1.0 / 256, 1e-5, ALU.mult, ALU.add))
        wait("scalar", sV, cv_ms)
        cs = SC(lambda s: s.activation(sm_rst[:, 0:1], sm_ms[:, 0:1], AF.Ln))
        wait("scalar", sS, cs)
        c_rstd = SC(lambda s: s.activation(
            sm_rst[:, 0:1], sm_rst[:, 0:1], AF.Exp, scale=-0.5))
        wait("vector", sS, c_rstd)
        wait("vector", sT, c_atp_slot[tl % 2])
        ao = a_out[:, (tl % 2) * 256:(tl % 2) * 256 + 256]
        c_aout = V(lambda v, ao=ao: v.scalar_tensor_tensor(
            ao, t2buf[:, :], sm_rst[:, 0:1], W2[:, :],
            ALU.mult, ALU.mult))
        return c_aout

    def emit_atp(h, tl, c_aout):
        atp = tl % 2
        ao = a_out[:, atp * 256:atp * 256 + 256]
        i, tp = take_slot()
        wait("tensor", sV, c_aout)
        for j in range(2):
            TP(tp[:, j * 128:(j + 1) * 128], ao[:, j * 128:(j + 1) * 128],
               inc=(j == 1))
        c_atp_slot[atp] = sT.n
        wait("scalar", sT, c_atp_slot[atp])
        dst = aT.rearrange("p (at s) -> p at s", s=OWN)[
            :, 2 * h:2 * h + 2, tl * 128:(tl + 1) * 128]
        c_atp_free[atp] = sS.bump()
        slot_guard[i] = [(sS, c_atp_free[atp])]
        E("scalar", lambda s, dst=dst, tp=tp, hh=sS.h: s.copy(
            dst, tp.rearrange("p (j c) -> p j c", c=128)[:, 0:2, :]
        ).then_inc(hh, 1))


    c_proj_evacs_s0 = sS.n
    carry = None
    pair_pend = []
    pair_pend_p = None
    c_kv_read = 0

    cc_fire = {"p": None}

    def weave_pop():
        nonlocal pair_pend_p
        spec = pair_pend.pop(0)
        i_, th_ = emit_proj_round(pair_pend_p, spec[0], spec[1], spec[2],
                                  dst="stage", defer=True)
        slot_pending[i_] = th_
        if not pair_pend:
            drain_all_slots()
            cc_fire["p"] = pair_pend_p
            pair_pend_p = None

    def fire_collective():
        p = cc_fire["p"]
        cc_fire["p"] = None
        for si in range(2):
            wait("gpsimd", sSTG[si], cc_st["guard"][si])
        sCC.bump()
        E("gpsimd", lambda g, p=p, hh=sCC.h: g.collective_compute(
            "AllGather", ALU.bypass, replica_groups=PAIR_GROUPS,
            ins=[cc_mine[p].opt()], outs=[cc_both[p].opt()]).then_inc(hh))

    for h in range(H):
        p = h // 2
        rank = h % 2
        if h > 0:
            wait("gpsimd", sT, PS_["wq_used"])
            c_wq[h] = w_slice_dma(wq_sb, wq_r, h, sWQ)
            first = True
            for j in range(2):
                for qc in range(NQC):
                    emit_proj_round(h, "q", j, qc, evac="scalar",
                                    extra_s_wait=(c_xnT_own if first else 0))
                    first = False
        c_q_evacs = sS.n
        kv_sem = sKV
        wait("sync", sCC, p + 1)
        wait("sync", sT, c_kv_read)
        dma("sync", sKV, kT[:, :],
            cc_both[p][rank * 128:(rank + 1) * 128, 0:2 * S])
        c_kv = dma("sync", sKV,
                   vbuf.rearrange("p (t c) -> p t c", c=264)[:, :, 0:256],
                   cc_both[p].rearrange("(r p) (c w) -> r p c w", p=128, w=256)[
                       rank, :, 2 * S // 256:, :])
        if rank == 0 and p + 1 < NP:
            pair_pend_p = p + 1
            pair_pend = ([(("k"), j, sc) for j in range(2) for sc in range(NSC)]
                         + [("v", skt, 0) for skt in range(TS)])
        if rank == 1 and p + 2 < NP:
            wait("gpsimd", sT, PS_["wk_used"])
            a2 = w_slice_dma(wk_sb, wk_r, p + 2, sWK)
            wait("gpsimd", sT, PS_["wv_used"])
            a3 = w_slice_dma(wv_sb, wv_r, p + 2, sWV)
            c_wkv[p + 2] = (a2, a3)
        if rank == 1 and cc_fire["p"] is not None:
            fire_collective()

        if carry is not None:
            hp, patp, cav = carry
            if len(patp) >= 2:
                emit_atp(hp, *patp.pop(0))
            ca = emit_combine(hp, TO - 1, cav)
            patp.append((TO - 1, ca))
            while patp:
                emit_atp(hp, *patp.pop(0))
            carry = None

        pend_atp = []
        prev_av = None
        for tl in range(TO):
            n_skt = 2 * tl + 2
            ng = (n_skt + 3) // 4
            avslot = tl % 2
            avbase = psB[:, 0:1024] if avslot == 0 else psB[:, 1024:2048]
            wait("tensor", kv_sem, c_kv)
            if h == 0:
                wait("tensor", sS, c_proj_evacs_s0)
            else:
                wait("tensor", sS, c_q_evacs)
            sc_order = [ng - 1] + list(range(ng - 1))
            c_expg2 = [[0] * ng, [0] * ng]
            cv_mask2 = [0, 0]
            av_next = [0, 0]

            def emit_scores(beta, g):
                pb = p_buf[:, beta * TS * 128:(beta + 1) * TS * 128]
                blocks = list(range(4 * g, min(4 * g + 4, n_skt)))
                i, ps = take_slot()
                cc = None
                for b in blocks:
                    cc = MM(ps[:, (b - blocks[0]) * 128:(b - blocks[0] + 1) * 128],
                            kT[:, beta * S + b * 128:beta * S + (b + 1) * 128],
                            qT[:, beta * OWN + tl * 128:beta * OWN + (tl + 1) * 128],
                            start=True, stop=True, inc=(b == blocks[-1]))
                w = len(blocks) * 128
                wait("scalar", sT, cc)
                c_exp = sS.bump()
                E("scalar", lambda s, ps=ps, pb=pb, g=g, w=w, hh=sS.h:
                  s.activation(pb[:, g * 512:g * 512 + w], ps[:, 0:w],
                               AF.Exp).then_inc(hh, 1))
                slot_guard[i] = [(sS, c_exp)]
                c_expg2[beta][g] = c_exp
                if g == ng - 1:
                    wait("vector", sS, c_exp)
                    if h == 0 and tl == 0 and beta == 0:
                        wait("vector", sMSK, c_masks)
                    cv_mask2[beta] = sV.bump()
                    E("vector", lambda v, pb=pb, g=g, w=w, tl=tl, hh=sV.h:
                      v.tensor_mul(pb[:, g * 512:g * 512 + w],
                                   pb[:, g * 512:g * 512 + w],
                                   masks[:, tl * 512:tl * 512 + w]).then_inc(hh, 1))

            def emit_av_group(beta, n_skt=n_skt, ng=ng, avbase=avbase,
                              avslot=avslot, c_expg2=c_expg2, cv_mask2=cv_mask2,
                              av_next=av_next):
                g = av_next[beta]
                av_next[beta] += 1
                pb = p_buf[:, beta * TS * 128:(beta + 1) * TS * 128]
                avs = avbase[:, beta * 512:beta * 512 + 512]
                blocks = list(range(4 * g, min(4 * g + 4, n_skt)))
                wait("tensor", sS, c_expg2[beta][g])
                if g == 0 and beta == 0:
                    wait("tensor", sV, c_av_free[avslot])
                if g == ng - 1:
                    wait("tensor", sV, cv_mask2[beta])
                for b in blocks:
                    MM(avs[:, 0:257], pb[:, b * 128:(b + 1) * 128],
                       vbuf[:, b * 264:b * 264 + 257],
                       start=(b == 0), stop=(b == n_skt - 1),
                       inc=(b == n_skt - 1))

            for g in sc_order:
                emit_scores(0, g)
                if prev_av is not None and prev_av[1][1] < prev_av[2]:
                    prev_av[0](1)
            while prev_av is not None and prev_av[1][1] < prev_av[2]:
                prev_av[0](1)
            c_av_prev = sT.n
            if pair_pend:
                weave_pop()
            if pair_pend:
                weave_pop()
            for idx, g in enumerate(sc_order):
                emit_scores(1, g)
                if idx >= 1 and av_next[0] < ng:
                    emit_av_group(0)
            if pair_pend:
                weave_pop()
            if pair_pend:
                weave_pop()
            while av_next[0] < ng:
                emit_av_group(0)
            prev_av = (emit_av_group, av_next, ng)

            if len(pend_atp) >= 2:
                emit_atp(h, *pend_atp.pop(0))
            if tl > 0:
                ca = emit_combine(h, tl - 1, c_av_prev)
                pend_atp.append((tl - 1, ca))
        while prev_av is not None and prev_av[1][1] < prev_av[2]:
            prev_av[0](1)
        c_av_tail = sT.n
        c_kv_read = sT.n
        while pair_pend:
            weave_pop()
        drain_all_slots()
        carry = (h, pend_atp, c_av_tail)
    hp, patp, cav = carry
    if len(patp) >= 2:
        emit_atp(hp, *patp.pop(0))
    ca = emit_combine(hp, TO - 1, cav)
    patp.append((TO - 1, ca))
    while patp:
        emit_atp(hp, *patp.pop(0))
    carry = None

    c_proj_done = PS_["last_proj"]

    gate_r = gate_wT.rearrange("(kt p) i -> p kt i", p=128)
    up_r = up_wT.rearrange("(kt p) i -> p kt i", p=128)
    down_r = down_wT.rearrange("(t p) hh -> p t hh", p=128)

    c_gu_free = [0, 0]
    c_dn_free = 0
    c_hdn_free = 0
    c_gu_psum_free = [0, 0]
    c_acc_hpt = [0] * KD
    c_sig_free = 0
    c_dw_free = [0, 0]

    c_gu_dma = [None] * NIT

    def gu_dma(it):
        buf = it % 2
        if it >= 2:
            wait("gpsimd", sT, c_gu_free[buf])
        else:
            wait("gpsimd", sT, c_proj_done)
        dma("gpsimd", sGU[buf],
            gate_sb.rearrange("p (b kt c) -> p b kt c", b=2, c=128)[:, buf],
            gate_r[:, :, it * 128:(it + 1) * 128])
        c_gu_dma[it] = dma("gpsimd", sGU[buf],
                           up_sb.rearrange("p (b kt c) -> p b kt c", b=2, c=128)[:, buf],
                           up_r[:, :, it * 128:(it + 1) * 128])

    for ch in range(NCH):
        it0 = ch * CHUNK
        nt = min(CHUNK, NIT - it0)
        if ch == 0:
            gu_dma(it0)
            if nt > 1:
                gu_dma(it0 + 1)
        wait("gpsimd", sT, c_proj_done if ch == 0 else c_dn_free)
        c_dn_dma = dma("gpsimd", sDN,
                       down_sb.rearrange("p (t hh) -> p t hh", hh=D)[:, 0:nt, :],
                       down_r[:, it0:it0 + nt, :])
        for itl in range(nt):
            it = it0 + itl
            buf = it % 2
            gps = psA if buf == 0 else psB
            wait("tensor", sGU[buf], c_gu_dma[it])
            wait("tensor", sV, c_gu_psum_free[buf])
            if it == 0:
                wait("tensor", sS, c_xnT_own)
                for gd in slot_guard:
                    for sem_, cnt_ in gd:
                        wait("tensor", sem_, cnt_)
                wait("tensor", sV, max(c_av_free))
            if it == 1:
                wait("tensor", sS, max(c_atp_free))
            last_g = last_u = None
            for half in range(2):
                tgt = gps[:, half * OWN:(half + 1) * OWN]
                src_w = gate_sb if half == 0 else up_sb
                for kt in range(KD):
                    for sqh in range(NQC):
                        sp = (kt == KD - 1 and sqh == NQC - 1)
                        MM(tgt[:, sqh * 512:(sqh + 1) * 512],
                           src_w[:, (buf * KD + kt) * 128:(buf * KD + kt + 1) * 128],
                           xnT_own[:, kt * OWN + sqh * 512:kt * OWN + sqh * 512 + 512],
                           start=(kt == 0), stop=sp, inc=sp)
                if half == 0:
                    last_g = sT.n
                else:
                    last_u = sT.n
            c_gu_free[buf] = last_u
            wait("scalar", sT, last_g)
            wait("scalar", sV, c_sig_free)
            cs_sig = sS.bump()
            E("scalar", lambda s, gps=gps, it=it, hh=sS.h: s.activation(
                silu_g[:, :], gps[:, 0:OWN], AF.Silu,
                scale=gate_sc[:, it:it + 1]).then_inc(hh, 1))
            wait("vector", sT, last_u)
            wait("vector", sS, cs_sig)
            if itl == 0:
                wait("vector", sT, c_hdn_free)
            c_sig_free = sV.bump()
            c_gu_psum_free[buf] = c_sig_free
            E("vector", lambda v, gps=gps, it=it, itl=itl, hh=sV.h:
              v.scalar_tensor_tensor(
                  hdn[:, itl * OWN:(itl + 1) * OWN], gps[:, OWN:2 * OWN],
                  up_sc[:, it:it + 1], silu_g[:, :],
                  ALU.mult, ALU.mult).then_inc(hh, 1))
            if it + 2 < NIT:
                gu_dma(it + 2)
        c_hdn_all = c_sig_free
        wait("tensor", sV, c_hdn_all)
        wait("tensor", sDN, c_dn_dma)
        for hpt in range(KD):
            pslot = hpt % 2
            dw = (psA if pslot == 0 else psB)[:, 0:OWN]
            wait("tensor", sV, max(c_dw_free[pslot], c_gu_psum_free[pslot]))
            last_c = None
            for itl in range(nt):
                for sqh in range(NQC):
                    sp = (itl == nt - 1 and sqh == NQC - 1)
                    last_c = MM(
                        dw[:, sqh * 512:(sqh + 1) * 512],
                        down_sb[:, itl * D + hpt * 128:itl * D + (hpt + 1) * 128],
                        hdn[:, itl * OWN + sqh * 512:itl * OWN + sqh * 512 + 512],
                        start=(itl == 0), stop=sp, inc=sp)
            wait("vector", sT, last_c)
            if ch > 0:
                wait("vector", sV, c_acc_hpt[hpt])
            dst = out_accT[:, hpt * OWN:(hpt + 1) * OWN]
            c_dw_free[pslot] = sV.bump()
            c_acc_hpt[hpt] = c_dw_free[pslot]
            if ch == 0:
                E("vector", lambda v, dst=dst, dw=dw, hpt=hpt, hh=sV.h:
                  v.tensor_scalar(dst, dw, down_sc[:, hpt:hpt + 1], None,
                                  ALU.mult).then_inc(hh, 1))
            else:
                E("vector", lambda v, dst=dst, dw=dw, hpt=hpt, hh=sV.h:
                  v.scalar_tensor_tensor(dst, dw, down_sc[:, hpt:hpt + 1], dst,
                                         ALU.mult, ALU.add).then_inc(hh, 1))
        c_gu_psum_free[0] = c_dw_free[0]
        c_gu_psum_free[1] = c_dw_free[1]
        c_dn_free = sT.n
        c_hdn_free = sT.n
    c_ffn_done = sV.n

    wo_r = wo.rearrange("(kt p) n -> p kt n", p=128)
    c_wo = [None] * (KD // 4)
    gb_slot_free = [0, 0]
    wait("tensor", sV, c_ffn_done)
    for g4 in range(KD // 4):
        for tl in range(TO):
            gslot = (g4 * TO + tl) % 2
            ps = psB[:, gslot * 512:gslot * 512 + 512]
            if gb_slot_free[gslot] > 0:
                wait("tensor", sS, gb_slot_free[gslot])
            for kk in range(4):
                hpt = g4 * 4 + kk
                TP(ps[:, kk * 128:(kk + 1) * 128],
                   out_accT[:, hpt * OWN + tl * 128:hpt * OWN + (tl + 1) * 128],
                   inc=(kk == 3))
            ctp = sT.n
            wait("scalar", sT, ctp)
            gb_slot_free[gslot] = sS.bump()
            E("scalar", lambda s, tl=tl, g4=g4, ps=ps, hh=sS.h: s.activation(
                g_sb[:, tl * D + g4 * 512:tl * D + (g4 + 1) * 512], ps,
                AF.Silu).then_inc(hh, 1))
        wait("gpsimd", sS, gb_slot_free[(g4 * TO + TO - 1) % 2])
        c_wo[g4] = dma("gpsimd", sWO[g4],
                       wo_sb.rearrange("p (kt n) -> p kt n", n=D)[:, g4 * 4:(g4 + 1) * 4, :],
                       wo_r[:, g4 * 4:(g4 + 1) * 4, :])
    c_g_done = sS.n


    c_fx = [None] * TO
    c_fcomb = [None] * TO
    c_store = [None] * TO
    c_fout_read = [0, 0]

    def emit_fx(tl):
        if tl >= 2:
            wait("sync", sV, c_fcomb[tl - 2])
        else:
            wait("sync", sS, c_g_done)
        c_fx[tl] = dma("sync", sX[tl % 2], fx_stage[:, (tl % 2) * D:(tl % 2 + 1) * D],
                       x_own[tl * 128:(tl + 1) * 128, :])

    emit_fx(0)
    if TO > 1:
        emit_fx(1)
    NFB = D // 512
    c_fout_read = [0, 0, 0, 0]
    for tl in range(TO):
        for fb in range(NFB):
            slot = fb % 4
            ps = psA[:, slot * 512:(slot + 1) * 512]
            if tl == 0 and fb == 0:
                wait("tensor", sS, c_g_done)
            wait("tensor", sV, c_fout_read[slot])
            last_c = None
            for at_ in range(KD):
                if at_ % 4 == 0 and fb == 0:
                    wait("tensor", sWO[at_ // 4], c_wo[at_ // 4])
                last_c = MM(
                    ps,
                    aT[:, at_ * OWN + tl * 128:at_ * OWN + (tl + 1) * 128],
                    wo_sb[:, at_ * D + fb * 512:at_ * D + fb * 512 + 512],
                    start=(at_ == 0), stop=(at_ == KD - 1), inc=(at_ == KD - 1))
            wait("vector", sT, last_c)
            wait("vector", sX[tl % 2], c_fx[tl])
            if tl >= 2 and fb == 0:
                wait("vector", sOUT[tl % 2], c_store[tl - 2])
            xsl = fx_stage[:, (tl % 2) * D + fb * 512:(tl % 2) * D + fb * 512 + 512]
            osl = fo_stage[:, (tl % 2) * D + fb * 512:(tl % 2) * D + fb * 512 + 512]
            gsl = g_sb[:, tl * D + fb * 512:tl * D + fb * 512 + 512]
            c_fout_read[slot] = V(lambda v, ps=ps, gsl=gsl, osl=osl:
                                  v.tensor_mul(osl, ps, gsl))
            wait("vector", sV, c_fout_read[slot])
            V(lambda v, osl=osl, xsl=xsl: v.tensor_add(osl, osl, xsl))
        c_fcomb[tl] = sV.n
        wait("sync", sV, c_fcomb[tl])
        c_store[tl] = dma("sync", sOUT[tl % 2], out_own[tl * 128:(tl + 1) * 128, :],
                          fo_stage[:, (tl % 2) * D:(tl % 2 + 1) * D])
        if tl + 2 < TO:
            emit_fx(tl + 2)
    wait("sync", sOUT[0], sOUT[0].n)
    wait("sync", sOUT[1], sOUT[1].n)
    wait("gpsimd", sOUT[0], sOUT[0].n)
    wait("gpsimd", sOUT[1], sOUT[1].n)

    with nc.Block() as block:
        @block.sync
        def _(e):
            for f in ops["sync"]:
                f(e)

        @block.gpsimd
        def _(e):
            for f in ops["gpsimd"]:
                f(e)

        @block.vector
        def _(e):
            for f in ops["vector"]:
                f(e)

        @block.scalar
        def _(e):
            for f in ops["scalar"]:
                f(e)

        @block.tensor
        def _(e):
            for f in ops["tensor"]:
                f(e)

    return nc



def make_masks(cfg, half):
    import ml_dtypes
    dv = derived(cfg)
    TO = dv["TO"]
    m = np.zeros((128, TO * 512), dtype=np.float32)
    for tl in range(TO):
        n_skt = 2 * tl + 2
        g_last = (n_skt - 1) // 4
        w = (n_skt - g_last * 4) * 128
        t = 2 * tl + half
        blk = np.zeros((128, 512), dtype=np.float32)
        for bi in range(g_last * 4, n_skt):
            col = (bi - g_last * 4) * 128
            if bi < t:
                blk[:, col:col + 128] = 1.0
            elif bi == t:
                tri = (np.arange(128)[:, None] <= np.arange(128)[None, :])
                blk[:, col:col + 128] = tri.astype(np.float32)
        m[:, tl * 512:tl * 512 + w] = blk[:, :w]
    return m.astype(ml_dtypes.bfloat16)


def make_in_maps(cfg, inputs):
    import ml_dtypes
    dv = derived(cfg)
    TS, TO, KD, NIT = dv["TS"], dv["TO"], dv["KD"], dv["NIT"]
    B = cfg["B"]

    H = cfg["H"]
    x = np.ascontiguousarray(np.asarray(inputs["L0_attn_x"], dtype=np.float32))
    wk_full = np.asarray(inputs["L0_attn_Wk"], np.float32)
    wv_full = np.asarray(inputs["L0_attn_Wv"], np.float32)

    def half_cols(w, half):
        return np.ascontiguousarray(np.concatenate(
            [w[:, (2 * p + half) * 256:(2 * p + half + 1) * 256]
             for p in range(H // 2)], axis=1))

    shared = dict(
        wq=np.ascontiguousarray(np.asarray(inputs["L0_attn_Wq"], np.float32)),
        wo=np.ascontiguousarray(np.asarray(inputs["L0_attn_Wo"], np.float32)),
        gate_wT=np.ascontiguousarray(np.asarray(inputs["L0_ffn_gate_w"], np.float32).T),
        up_wT=np.ascontiguousarray(np.asarray(inputs["L0_ffn_up_w"], np.float32).T),
        down_wT=np.ascontiguousarray(np.asarray(inputs["L0_ffn_down_w"], np.float32).T),
        gate_s2=np.ascontiguousarray(
            np.asarray(inputs["L0_ffn_gate_s"], np.float32).reshape(NIT, 128).T),
        up_s2=np.ascontiguousarray(
            np.asarray(inputs["L0_ffn_up_s"], np.float32).reshape(NIT, 128).T),
        down_s2=np.ascontiguousarray(
            np.asarray(inputs["L0_ffn_down_s"], np.float32).reshape(KD, 128).T),
        lam_row=np.concatenate([
            np.asarray(inputs["L0_attn_lambda_q1"], np.float32),
            np.asarray(inputs["L0_attn_lambda_k1"], np.float32),
            np.asarray(inputs["L0_attn_lambda_q2"], np.float32),
            np.asarray(inputs["L0_attn_lambda_k2"], np.float32)]).reshape(1, 512),
        ones_row=np.ones((1, 128), dtype=np.float32),
        subln_w=np.ascontiguousarray(np.broadcast_to(
            np.asarray(inputs["L0_attn_subln_weight"], np.float32)[None, :],
            (128, 256))),
        ident_in=np.eye(128, dtype=np.float32),
        vones_in=np.ones((128, TS), dtype=ml_dtypes.bfloat16),
    )
    masks_by_half = [make_masks(cfg, 0), make_masks(cfg, 1)]

    in_maps = []
    for c in range(2 * B):
        b, half = c // 2, c % 2
        own_rows = np.concatenate(
            [x[b, (2 * tl + half) * 128:(2 * tl + half + 1) * 128, :]
             for tl in range(TO)], axis=0)
        m = dict(shared)
        m["x_own"] = np.ascontiguousarray(own_rows)
        m["x_kv"] = np.ascontiguousarray(x[b])
        m["masks_in"] = masks_by_half[half]
        m["wk"] = half_cols(wk_full, half)
        m["wv"] = half_cols(wv_full, half)
        in_maps.append(m)
    return in_maps


def assemble_out(cfg, results):
    dv = derived(cfg)
    S, D, TO = dv["S"], dv["D"], dv["TO"]
    B = cfg["B"]
    out = np.empty((B, S, D), dtype=np.float32)
    for c in range(2 * B):
        b, half = c // 2, c % 2
        r = results[c]["out_own"]
        for tl in range(TO):
            out[b, (2 * tl + half) * 128:(2 * tl + half + 1) * 128, :] = \
                r[tl * 128:(tl + 1) * 128, :]
    return out


_CACHE = {}


def kernel(**inputs):
    cfg = default_cfg()
    if "nc" not in _CACHE:
        _CACHE["nc"] = build_nc(cfg)
    in_maps = make_in_maps(cfg, inputs)
    res = run_bass_kernel_spmd(_CACHE["nc"], in_maps, core_ids=list(range(8)))
    return assemble_out(cfg, res.results)
